# version 05
# baseline (speedup 1.0000x reference)
import contextlib
import numpy as np
import ml_dtypes
import concourse.bass as bass
import concourse.mybir as mybir
from concourse.bass_utils import run_bass_kernel_spmd

F32 = mybir.dt.float32
BF16 = mybir.dt.bfloat16
AF = mybir.ActivationFunctionType
ALU = mybir.AluOpType
AX = mybir.AxisListType

D = 1024
E = 2048
EH = 1024
NST = 16
RNK = 64
HF = 4096
TP = 256
EPS = 1e-5
ENGINES = ("sync", "scalar", "vector", "tensor", "gpsimd")

BF16_NP = ml_dtypes.bfloat16


class Sem:
    def __init__(self, nc, cm, name):
        self.h = cm.enter_context(nc.semaphore(name))
        self.name = name
        self.n = 0

    def now(self):
        return (self, self.n)


class Plan:
    def __init__(self):
        self.ops = {e: [] for e in ENGINES}
        self.base = []

    def do(self, eng, fn, waits=(), inc=None, incn=1):
        waits = tuple((s, v) for (s, v) in tuple(waits) + tuple(self.base) if v > 0)
        if inc is not None:
            inc.n += incn
        self.ops[eng].append((waits, fn, (inc, incn) if inc is not None else None))

    def simulate(self):
        counts = {}
        ptr = {e: 0 for e in ENGINES}
        total = sum(len(v) for v in self.ops.values())
        done = 0
        progress = True
        while progress:
            progress = False
            for e in ENGINES:
                while ptr[e] < len(self.ops[e]):
                    waits, _, inc = self.ops[e][ptr[e]]
                    if any(counts.get(s.name, 0) < v for s, v in waits):
                        break
                    if inc is not None:
                        s, n = inc
                        counts[s.name] = counts.get(s.name, 0) + n
                    ptr[e] += 1
                    done += 1
                    progress = True
        if done != total:
            stuck = {e: (ptr[e], len(self.ops[e])) for e in ENGINES}
            detail = {}
            for e in ENGINES:
                if ptr[e] < len(self.ops[e]):
                    waits, _, _ = self.ops[e][ptr[e]]
                    detail[e] = [(s.name, v, counts.get(s.name, 0)) for s, v in waits]
            raise RuntimeError(f"plan deadlock: {stuck} waiting {detail}")

    def attach(self, block):
        def mk(name):
            def body(e):
                last = {}
                for waits, fn, inc in self.ops[name]:
                    for s, v in waits:
                        if v > last.get(s.name, -1):
                            e.wait_ge(s.h, v)
                            last[s.name] = v
                    ins = fn(e)
                    if inc is not None:
                        s, n = inc
                        ins.then_inc(s.h, n)
            return body
        block.sync(mk("sync"))
        block.scalar(mk("scalar"))
        block.vector(mk("vector"))
        block.tensor(mk("tensor"))
        block.gpsimd(mk("gpsimd"))


def _sb(nc, cm, name, shape, dt):
    return cm.enter_context(nc.sbuf_tensor(name, shape, dt))


def _ps(nc, cm, name, dt=F32):
    n = 512 if dt == F32 else 1024
    return cm.enter_context(nc.psum_tensor(name, [128, n], dt))



def build_l1(L=8192):
    NC = L // TP
    nc = bass.Bass("TRN2", target_bir_lowering=False, debug=False, detect_race_conditions=False)
    dr = lambda n, s, dt, k: nc.dram_tensor(n, s, dt, kind=k).ap()
    xd = dr("xd", [L, D], F32, "ExternalInput")
    win = dr("win", [E + EH, D], F32, "ExternalInput")
    convw = dr("convw", [E, 4], F32, "ExternalInput")
    convb = dr("convb", [E, 1], F32, "ExternalInput")
    wx = dr("wx", [RNK + 2 * NST, E], F32, "ExternalInput")
    wdt = dr("wdt", [EH, RNK], F32, "ExternalInput")
    bdt = dr("bdt", [EH, 1], F32, "ExternalInput")
    alog = dr("alog", [EH, NST], F32, "ExternalInput")
    dsk = dr("dsk", [EH, 1], F32, "ExternalInput")
    wout = dr("wout", [D, EH], F32, "ExternalInput")
    gvec = dr("gvec", [D, 1], F32, "ExternalInput")
    bvec = dr("bvec", [D, 1], F32, "ExternalInput")
    eyef = dr("eyef", [128, 128], F32, "ExternalInput")
    eyeb = dr("eyeb", [128, 128], BF16, "ExternalInput")
    part = dr("part", [L, D], F32, "ExternalOutput")
    bscr = dr("bscr", [2 * NST, L], BF16, "Internal")

    NU = E // 128
    NE = (E + EH) // 128
    ND = EH // 128
    NK = D // 128

    cm = contextlib.ExitStack()
    with cm:
        winT = [_sb(nc, cm, f"winT{k}", [128, NE * 128], BF16) for k in range(NK)]
        woutT = [_sb(nc, cm, f"woutT{k}", [128, D], BF16) for k in range(ND)]
        wxT = _sb(nc, cm, "wxT", [128, NU * 96], BF16)
        wdtT = _sb(nc, cm, "wdtT", [64, EH], BF16)
        asc = _sb(nc, cm, "asc", [128, ND * NST], F32)
        gv_s = _sb(nc, cm, "gv_s", [128, NK], F32)
        bvb = _sb(nc, cm, "bvb", [128, NK], BF16)
        bv_s = _sb(nc, cm, "bv_s", [128, NK], F32)
        bdt_s = _sb(nc, cm, "bdt_s", [128, ND], F32)
        dsk_s = _sb(nc, cm, "dsk_s", [128, ND], F32)
        cb_s = _sb(nc, cm, "cb_s", [128, NU], F32)
        cw_s = _sb(nc, cm, "cw_s", [128, NU * 4], F32)
        wb_s = _sb(nc, cm, "wb_s", [128, NE], F32)
        eye_f = _sb(nc, cm, "eye_f", [128, 128], F32)
        eye_b = _sb(nc, cm, "eye_b", [128, 128], BF16)
        xa = _sb(nc, cm, "xa", [128, 2 * 2 * D], F32)
        ln_s = _sb(nc, cm, "ln_s", [128, 4], F32)
        ln_q = _sb(nc, cm, "ln_q", [128, 4], F32)
        ln_m = _sb(nc, cm, "ln_m", [128, 4], F32)
        ln_t = _sb(nc, cm, "ln_t", [128, 4], F32)
        ln_v = _sb(nc, cm, "ln_v", [128, 4], F32)
        q_s = _sb(nc, cm, "q_s", [128, 4], F32)
        nmq = _sb(nc, cm, "nmq", [128, 4], F32)
        xnT = _sb(nc, cm, "xnT", [128, NK * TP], BF16)
        uex = _sb(nc, cm, "uex", [128, 2 * (TP + 3)], F32)
        uacc = _sb(nc, cm, "uacc", [128, 2 * TP], F32)
        ctmp = _sb(nc, cm, "ctmp", [128, TP], F32)
        halo = _sb(nc, cm, "halo", [128, NU * 3], F32)
        ub_s = _sb(nc, cm, "ub_s", [128, (NU - ND) * TP], BF16)
        uh = _sb(nc, cm, "uh", [128, 2 * ND * TP], BF16)
        dt_s = _sb(nc, cm, "dt_s", [128, 2 * ND * TP], F32)
        dtu_s = _sb(nc, cm, "dtu_s", [128, 2 * ND * TP], BF16)
        szs = _sb(nc, cm, "szs", [128, 2 * ND * TP], BF16)
        dtr_b = _sb(nc, cm, "dtr_b", [64, TP], BF16)
        bc_sb = _sb(nc, cm, "bc_sb", [128, TP], BF16)
        repB = _sb(nc, cm, "repB", [128, NST * TP], BF16)
        repC = _sb(nc, cm, "repC", [128, NST * TP], BF16)
        ball = _sb(nc, cm, "ball", [128, NST * TP], BF16)
        hall = _sb(nc, cm, "hall", [128, NST * TP], BF16)
        zall = _sb(nc, cm, "zall", [128, NST * TP], BF16)
        zred = _sb(nc, cm, "zred", [128, TP], F32)
        a_s = _sb(nc, cm, "a_s", [128, 2 * TP], F32)
        sp4 = _sb(nc, cm, "sp4", [128, 4 * TP], F32)
        y_t = _sb(nc, cm, "y_t", [128, TP], F32)
        yg_s = _sb(nc, cm, "yg_s", [128, ND * TP], BF16)
        ofm = _sb(nc, cm, "ofm", [128, 2 * TP], F32)
        otm = _sb(nc, cm, "otm", [128, 2 * D], F32)
        states = _sb(nc, cm, "states", [128, 128], F32)
        zzpad = _sb(nc, cm, "zzpad", [128, 96], F32)
        pA = _ps(nc, cm, "pA")
        pB = _ps(nc, cm, "pB")
        pTa = _ps(nc, cm, "pTa", BF16)
        pTb = _ps(nc, cm, "pTb", BF16)
        pWx = _ps(nc, cm, "pWx")
        pM = _ps(nc, cm, "pM")

        P = Plan()
        S = {k: Sem(nc, cm, k) for k in ("dx0", "dx1", "dwt", "dcn", "bcw", "rep", "po",
                                         "ps", "pv", "pt", "pg")}
        dxp = (S["dx0"], S["dx1"])
        dwt, dcn, bcw, rep, po = (S[k] for k in ("dwt", "dcn", "bcw", "rep", "po"))
        ps_, pv, pt_, pg = (S[k] for k in ("ps", "pv", "pt", "pg"))
        mark = {}
        ttsm = {}
        banks = {b: (ps_, 0) for b in ("pA", "pB", "pTa", "pTb", "pWx", "pM")}

        def bk_acq(b):
            s, v = banks[b]
            return (s, v)

        def bk_rel(b, sem):
            banks[b] = (sem, sem.n)

        tcyc = [0]
        tcycf = [0]

        def pT_next():
            t = ("pTa", "pTb")[tcyc[0] % 2]
            tcyc[0] += 1
            return t

        def pF_next():
            t = ("pA", "pB")[tcycf[0] % 2]
            tcycf[0] += 1
            return t

        def D_(eng, fn, waits=(), inc=None, incn=1):
            P.do(eng, fn, waits, inc, incn)

        def dma(e, dst, src):
            return e.dma_start(out=dst, in_=src)

        D_("sync", lambda e: dma(e, eye_f[:], eyef[:]), inc=dcn, incn=16)
        D_("sync", lambda e: dma(e, eye_b[:], eyeb[:]), inc=dcn, incn=16)
        for k in range(NK):
            D_("sync", lambda e, k=k: dma(e, gv_s[:, k:k + 1], gvec[k * 128:(k + 1) * 128, :]), inc=dcn, incn=16)
            D_("sync", lambda e, k=k: dma(e, bv_s[:, k:k + 1], bvec[k * 128:(k + 1) * 128, :]), inc=dcn, incn=16)
        for j in range(ND):
            D_("sync", lambda e, j=j: dma(e, bdt_s[:, j:j + 1], bdt[j * 128:(j + 1) * 128, :]), inc=dcn, incn=16)
            D_("sync", lambda e, j=j: dma(e, dsk_s[:, j:j + 1], dsk[j * 128:(j + 1) * 128, :]), inc=dcn, incn=16)
            D_("sync", lambda e, j=j: dma(e, asc[:, j * NST:(j + 1) * NST], alog[j * 128:(j + 1) * 128, :]), inc=dcn, incn=16)
        for t in range(NU):
            D_("sync", lambda e, t=t: dma(e, cb_s[:, t:t + 1], convb[t * 128:(t + 1) * 128, :]), inc=dcn, incn=16)
            D_("sync", lambda e, t=t: dma(e, cw_s[:, 4 * t:4 * t + 4], convw[t * 128:(t + 1) * 128, :]), inc=dcn, incn=16)
        dcn_all = dcn.now()

        D_("scalar", lambda e: e.activation(sp4[:, 0:ND * NST], asc[:], AF.Exp), waits=[dcn_all], inc=ps_)
        D_("vector", lambda e: e.tensor_scalar(asc[:], sp4[:, 0:ND * NST], -1.0, None, ALU.mult),
           waits=[ps_.now()], inc=pv)
        D_("vector", lambda e: e.tensor_copy(bvb[:], bv_s[:]), waits=[dcn_all], inc=pv)
        D_("vector", lambda e: e.memset(states[:], 0.0), inc=pv)
        D_("gpsimd", lambda e: e.memset(halo[:], 0.0), inc=pg)

        stage_gate = [0]

        def transpose_evac(src_ap, dst_ap, ident, extra_w=()):
            bname = pT_next()
            pT = pTa if bname == "pTa" else pTb
            D_("tensor", lambda e, pT=pT: e.transpose(pT[:, 0:src_ap.shape[0]], src_ap, ident),
               waits=[bk_acq(bname)] + list(extra_w), inc=pt_)
            D_("scalar", lambda e, pT=pT: e.copy(dst_ap, pT[:, 0:src_ap.shape[0]]),
               waits=[pt_.now()], inc=ps_)
            bk_rel(bname, ps_)

        def wprep(dst_sel, src, rows, row_len, tag):
            nkt = row_len // 128
            for r in range(rows):
                slot = r % 2
                st_f = xa[:, slot * 2 * D: slot * 2 * D + row_len]
                st_b = xnT[:, slot * D: slot * D + row_len]
                D_("sync", lambda e, st_f=st_f, r=r: dma(e, st_f, src[r * 128:(r + 1) * 128, :]),
                   waits=[(pt_, stage_gate[0]), (dwt, dwt.n)], inc=dwt, incn=16)
                D_("vector", lambda e, st_b=st_b, st_f=st_f: e.tensor_copy(st_b, st_f),
                   waits=[dwt.now(), (pt_, stage_gate[0])], inc=pv)
                pv_mark = pv.now()
                for k in range(nkt):
                    dst, dcol = dst_sel(r, k)
                    transpose_evac(st_b[:, k * 128:(k + 1) * 128], dst[:, dcol:dcol + 128],
                                   eye_b[:], extra_w=[pv_mark, dcn_all])
                stage_gate[0] = pt_.n

        wprep(lambda r, k: (winT[k], r * 128), win, NE, D, "w_in")
        wprep(lambda r, k: (woutT[k], r * 128), wout, ND, D, "w_out")

        D_("sync", lambda e: dma(e, xa[0:96, 0:E], wx[:, :]),
           waits=[(pt_, stage_gate[0]), (dwt, dwt.n)], inc=dwt, incn=16)
        D_("vector", lambda e: e.tensor_copy(xnT[0:96, 0:E], xa[0:96, 0:E]),
           waits=[dwt.now(), (pt_, stage_gate[0])], inc=pv)
        pv_m = pv.now()
        for k in range(NU):
            transpose_evac(xnT[0:96, k * 128:(k + 1) * 128], wxT[:, k * 96:(k + 1) * 96],
                           eye_b[0:96, 0:96], extra_w=[pv_m, dcn_all])
        stage_gate[0] = pt_.n
        for j in range(ND):
            slot = j % 2
            st_f = xa[:, slot * 2 * D: slot * 2 * D + 64]
            st_b = xnT[:, slot * D: slot * D + 64]
            D_("sync", lambda e, st_f=st_f, j=j: dma(e, st_f, wdt[j * 128:(j + 1) * 128, :]),
               waits=[(pt_, stage_gate[0]), (dwt, dwt.n)], inc=dwt, incn=16)
            D_("vector", lambda e, st_b=st_b, st_f=st_f: e.tensor_copy(st_b, st_f),
               waits=[dwt.now(), (pt_, stage_gate[0])], inc=pv)
            pv_m = pv.now()
            bname = pT_next()
            pT = pTa if bname == "pTa" else pTb
            D_("tensor", lambda e, pT=pT, st_b=st_b: e.transpose(pT[0:64, 0:128], st_b, eye_b[:]),
               waits=[bk_acq(bname), pv_m, dcn_all], inc=pt_)
            mark[("w_dt", j)] = pt_.n
            D_("scalar", lambda e, pT=pT, j=j: e.copy(wdtT[0:64, j * 128:(j + 1) * 128], pT[0:64, 0:128]),
               waits=[pt_.now()], inc=ps_)
            bk_rel(bname, ps_)
            stage_gate[0] = pt_.n

        for eidx in range(NE):
            for k in range(NK):
                D_("tensor", lambda e, eidx=eidx, k=k:
                   e.matmul(pM[:, 0:1], winT[k][:, eidx * 128:(eidx + 1) * 128], bvb[:, k:k + 1],
                            start=(k == 0), stop=(k == NK - 1)),
                   waits=[bk_acq("pM"), (pv, pv.n), (ps_, ps_.n)] if k == 0 else (),
                   inc=pt_ if k == NK - 1 else None)
            D_("scalar", lambda e, eidx=eidx: e.copy(wb_s[:, eidx:eidx + 1], pM[:, 0:1]),
               waits=[pt_.now()], inc=ps_)
            bk_rel("pM", ps_)

        P.base = [dcn_all, (ps_, ps_.n), (pv, pv.n)]
        prep_t = (pt_, pt_.n)

        def x_dma(c):
            if c >= NC:
                return
            slot = c % 2
            dxs_ = dxp[c % 2]
            w = [(pt_, mark.get(("xT", c - 2), 0)), prep_t]
            for tau in range(2):
                D_("sync", lambda e, c=c, slot=slot, tau=tau:
                   dma(e, xa[:, slot * 2 * D + tau * D: slot * 2 * D + (tau + 1) * D],
                       xd[c * TP + tau * 128: c * TP + (tau + 1) * 128, :]),
                   waits=w, inc=dxs_, incn=16)
            mark[("xdma", c)] = (dxs_, dxs_.n)

        def chain(c):
            slot = c % 2
            th = [[] for _ in range(9)]

            def sl_ln():
                for tau in range(2):
                    xt = xa[:, slot * 2 * D + tau * D: slot * 2 * D + (tau + 1) * D]
                    col = slot * 2 + tau
                    D_("scalar", lambda e, xt=xt, col=col:
                       e.activation(sp4[:, 0:D], xt, AF.Square, accum_out=ln_q[:, col:col + 1]),
                       waits=[mark[("xdma", c)], ps_.now(),
                              (pg, mark.get(("dtadd", c - 1, ND - 1), 0))], inc=ps_)
                    D_("vector", lambda e, xt=xt, col=col:
                       e.tensor_reduce(ln_s[:, col:col + 1], xt, AX.X, ALU.add),
                       waits=[mark[("xdma", c)], (pg, mark.get(("norm", c - 1), 0))], inc=pv)
                cs = slice(slot * 2, slot * 2 + 2)
                D_("gpsimd", lambda e, cs=cs: e.tensor_scalar(ln_m[:, cs], ln_s[:, cs], 1.0 / D, None, ALU.mult),
                   waits=[ps_.now(), pv.now()], inc=pg)
                D_("gpsimd", lambda e, cs=cs: e.tensor_scalar(ln_v[:, cs], ln_q[:, cs], 1.0 / D, None, ALU.mult), inc=pg)
                D_("gpsimd", lambda e, cs=cs: e.tensor_tensor(ln_t[:, cs], ln_m[:, cs], ln_m[:, cs], ALU.mult), inc=pg)
                D_("gpsimd", lambda e, cs=cs: e.tensor_tensor(ln_v[:, cs], ln_v[:, cs], ln_t[:, cs], ALU.subtract), inc=pg)
                D_("gpsimd", lambda e: e.tensor_scalar(ln_t[:, cs], ln_v[:, cs], 1.0 + EPS, EPS * EPS, ALU.mult, ALU.add), inc=pg)
                D_("scalar", lambda e, cs=cs: e.activation(ln_v[:, cs], ln_t[:, cs], AF.Ln), waits=[pg.now()], inc=ps_)
                D_("scalar", lambda e: e.activation(q_s[:, cs], ln_v[:, cs], AF.Exp, scale=-0.5), inc=ps_)
                D_("gpsimd", lambda e: e.tensor_tensor(nmq[:, cs], ln_m[:, cs], q_s[:, cs], ALU.mult),
                   waits=[ps_.now()], inc=pg)
                D_("gpsimd", lambda e: e.tensor_scalar(nmq[:, cs], nmq[:, cs], -1.0, None, ALU.mult), inc=pg)
                for tau in range(2):
                    xt = xa[:, slot * 2 * D + tau * D: slot * 2 * D + (tau + 1) * D]
                    col = slot * 2 + tau
                    D_("gpsimd", lambda e, xt=xt, col=col:
                       e.tensor_scalar(xt, xt, q_s[:, col:col + 1], nmq[:, col:col + 1], ALU.mult, ALU.add),
                       inc=pg)
                mark[("norm", c)] = pg.n

            def sl_xt():
                for tau in range(2):
                    for k in range(NK):
                        src = xa[:, slot * 2 * D + tau * D + k * 128: slot * 2 * D + tau * D + (k + 1) * 128]
                        bname = pF_next()
                        pT = pA if bname == "pA" else pB
                        D_("tensor", lambda e, pT=pT, src=src: e.transpose(pT[:, 0:128], src, eye_f[:]),
                           waits=[bk_acq(bname), (pg, mark[("norm", c)])], inc=pt_)
                        D_("scalar", lambda e, pT=pT, tau=tau, k=k:
                           e.activation(xnT[:, k * TP + tau * 128: k * TP + tau * 128 + 128],
                                        pT[:, 0:128], AF.Identity, scale=gv_s[:, k:k + 1]),
                           waits=[pt_.now(), (pt_, mark.get(("win_mm", c - 1), 0))], inc=ps_)
                        bk_rel(bname, ps_)
                mark[("xT", c)] = pt_.n
                mark[("xnT", c)] = ps_.n

            def win_etile(eidx):
                bname = ("pA", "pB")[eidx % 2]
                pp = pA if bname == "pA" else pB
                for k in range(NK):
                    w = [(ps_, mark[("xnT", c)]), bk_acq(bname)] if k == 0 else ()
                    D_("tensor", lambda e, pp=pp, eidx=eidx, k=k:
                       e.matmul(pp[:, 0:TP], winT[k][:, eidx * 128:(eidx + 1) * 128],
                                xnT[:, k * TP:(k + 1) * TP], start=(k == 0), stop=(k == NK - 1)),
                       waits=w, inc=pt_ if k == NK - 1 else None)
                mm_mark = pt_.n
                if eidx == NE - 1:
                    mark[("win_mm", c)] = pt_.n
                if eidx < NU:
                    ue = uex[:, (eidx % 2) * (TP + 3): (eidx % 2 + 1) * (TP + 3)]
                    ua = uacc[:, (eidx % 2) * TP: (eidx % 2 + 1) * TP]
                    D_("gpsimd", lambda e, ue=ue, eidx=eidx:
                       e.tensor_copy(ue[:, 0:3], halo[:, 3 * eidx:3 * eidx + 3]), inc=pg)
                    hr = pg.now()
                    D_("scalar", lambda e, ue=ue, pp=pp, eidx=eidx:
                       e.activation(ue[:, 3:3 + TP], pp[:, 0:TP], AF.Identity, bias=wb_s[:, eidx:eidx + 1]),
                       waits=[(pt_, mm_mark), hr], inc=ps_)
                    bk_rel(bname, ps_)
                    D_("gpsimd", lambda e, ua=ua, ue=ue, eidx=eidx:
                       e.tensor_scalar(ua, ue[:, 0:TP], cw_s[:, 4 * eidx:4 * eidx + 1], None, ALU.mult),
                       waits=[ps_.now()], inc=pg)
                    for kk in range(1, 4):
                        D_("gpsimd", lambda e, ue=ue, eidx=eidx, kk=kk:
                           e.tensor_scalar(ctmp[:], ue[:, kk:kk + TP], cw_s[:, 4 * eidx + kk:4 * eidx + kk + 1],
                                           None, ALU.mult), inc=pg)
                        D_("gpsimd", lambda e, ua=ua: e.tensor_tensor(ua, ctmp[:], ua, ALU.add), inc=pg)
                    D_("gpsimd", lambda e, ue=ue, eidx=eidx:
                       e.tensor_copy(halo[:, 3 * eidx:3 * eidx + 3], ue[:, TP:TP + 3]), inc=pg)
                    mark[("conv", c, eidx)] = pg.n
                    if eidx < ND:
                        j = eidx
                        D_("scalar", lambda e, ua=ua, j=j:
                           e.activation(uh[:, slot * ND * TP + j * TP: slot * ND * TP + (j + 1) * TP],
                                        ua, AF.Silu, bias=cb_s[:, j:j + 1]),
                           waits=[(pg, mark[("conv", c, eidx)]),
                                  (pv, mark.get(("yD", c - 2, j), 0)),
                                  (pg, mark.get(("dt", c - 2, j), 0))], inc=ps_)
                        mark[("uh", c, j)] = ps_.n
                        mark[("ub", c, eidx)] = ps_.n
                    else:
                        D_("scalar", lambda e, ua=ua, eidx=eidx:
                           e.activation(ub_s[:, (eidx - ND) * TP:(eidx - ND + 1) * TP], ua, AF.Silu,
                                        bias=cb_s[:, eidx:eidx + 1]),
                           waits=[(pg, mark[("conv", c, eidx)]),
                                  (pt_, mark.get(("wx_mm", c - 1), 0))], inc=ps_)
                        mark[("ub", c, eidx)] = ps_.n
                else:
                    zi = eidx - NU
                    D_("scalar", lambda e, pp=pp, eidx=eidx, zi=zi:
                       e.activation(szs[:, slot * ND * TP + zi * TP: slot * ND * TP + (zi + 1) * TP],
                                    pp[:, 0:TP], AF.Silu, bias=wb_s[:, eidx:eidx + 1]),
                       waits=[(pt_, mm_mark), (pv, mark.get(("yg", c - 2, zi), 0))], inc=ps_)
                    bk_rel(bname, ps_)
                    if zi == ND - 1:
                        mark[("sz", c)] = ps_.n

            def sl_wx():
                for k in range(NU):
                    w = [(ps_, mark[("ub", c, k)])]
                    if k == 0:
                        w.append(bk_acq("pWx"))
                    rhs = (uh[:, slot * ND * TP + k * TP: slot * ND * TP + (k + 1) * TP]
                           if k < ND else ub_s[:, (k - ND) * TP:(k - ND + 1) * TP])
                    D_("tensor", lambda e, k=k, rhs=rhs:
                       e.matmul(pWx[0:96, 0:TP], wxT[:, k * 96:(k + 1) * 96],
                                rhs, start=(k == 0), stop=(k == NU - 1)),
                       waits=w, inc=pt_ if k == NU - 1 else None)
                mark[("wx_mm", c)] = pt_.n
                D_("scalar", lambda e: e.copy(dtr_b[0:64, :], pWx[0:64, 0:TP]),
                   waits=[(pt_, mark[("wx_mm", c)])], inc=ps_)
                mark[("dtr", c)] = ps_.n
                D_("scalar", lambda e: e.copy(bc_sb[64:96, :], pWx[64:96, 0:TP]),
                   waits=[(bcw, mark.get(("bcw", c - 1), 0))], inc=ps_)
                bk_rel("pWx", ps_)
                mark[("bc", c)] = ps_.n
                D_("sync", lambda e: dma(e, bscr[:, c * TP:(c + 1) * TP], bc_sb[64:96, :]),
                   waits=[(ps_, mark[("bc", c)])], inc=bcw, incn=16)
                mark[("bcw", c)] = bcw.n

            def wdt_tile(j):
                bname = ("pA", "pB")[j % 2]
                pp = pA if bname == "pA" else pB
                D_("tensor", lambda e, pp=pp, j=j:
                   e.matmul(pp[:, 0:TP], wdtT[0:64, j * 128:(j + 1) * 128], dtr_b[0:64, :],
                            start=True, stop=True),
                   waits=[(ps_, mark[("dtr", c)]), bk_acq(bname)], inc=pt_)
                mmk = pt_.n
                s0 = sp4[:, (j % 2) * 2 * TP:(j % 2) * 2 * TP + TP]
                s1 = sp4[:, (j % 2) * 2 * TP + TP:(j % 2) * 2 * TP + 2 * TP]
                D_("scalar", lambda e, pp=pp, j=j, s0=s0:
                   e.activation(s0, pp[:, 0:TP], AF.Abs, bias=bdt_s[:, j:j + 1]),
                   waits=[(pt_, mmk), (pg, mark.get(("dtadd", c, j - 2), 0))], inc=ps_)
                D_("scalar", lambda e, s0=s0, s1=s1: e.activation(s1, s0, AF.Exp, scale=-1.0), inc=ps_)
                D_("scalar", lambda e, s0=s0, s1=s1: e.activation(s0, s1, AF.Ln, bias=1.0), inc=ps_)
                D_("scalar", lambda e, pp=pp, j=j, s1=s1:
                   e.activation(s1, pp[:, 0:TP], AF.Relu, bias=bdt_s[:, j:j + 1]), inc=ps_)
                mark[("sp", c, j)] = ps_.n
                bk_rel(bname, ps_)
                dslot = c % 2
                dts = dt_s[:, dslot * ND * TP + j * TP: dslot * ND * TP + (j + 1) * TP]
                dtus = dtu_s[:, dslot * ND * TP + j * TP: dslot * ND * TP + (j + 1) * TP]
                D_("gpsimd", lambda e, s0=s0, s1=s1, dts=dts: e.tensor_tensor(dts, s0, s1, ALU.add),
                   waits=[(ps_, mark[("sp", c, j)]), (ps_, mark.get(("aexp", c - 1), 0))], inc=pg)
                mark[("dtadd", c, j)] = pg.n
                D_("gpsimd", lambda e, dts=dts, dtus=dtus, j=j:
                   e.tensor_tensor(dtus, dts, uh[:, dslot * ND * TP + j * TP: dslot * ND * TP + (j + 1) * TP],
                                   ALU.mult),
                   waits=[(ps_, mark[("uh", c, j)]), (pv, mark.get(("p1", c - 1, j), 0))], inc=pg)
                mark[("dt", c, j)] = pg.n

            def sl_reps():
                wrep = [(bcw, mark[("bcw", c)]), (pv, mark.get(("repfree", c - 1), 0))]
                D_("sync", lambda e:
                   dma(e, repB[:].rearrange("p (n t) -> p n t", n=NST),
                       bscr[0:NST, c * TP:(c + 1) * TP][None].broadcast_to((128, NST, TP))),
                   waits=wrep, inc=rep, incn=16)
                D_("sync", lambda e:
                   dma(e, repC[:].rearrange("p (n t) -> p n t", n=NST),
                       bscr[NST:2 * NST, c * TP:(c + 1) * TP][None].broadcast_to((128, NST, TP))),
                   waits=wrep, inc=rep, incn=16)
                mark[("rep", c)] = rep.n

            th[0].append(sl_ln)
            th[1].append(sl_xt)
            for i in range(NE):
                th[2 + i // 6].append(lambda i=i: win_etile(i))
            th[6].append(sl_wx)
            for j in range(ND):
                th[7].append(lambda j=j: wdt_tile(j))
            th[8].append(sl_reps)
            return th

        def scan_dtile(c, j):
            dslot = c % 2
            dts = dt_s[:, dslot * ND * TP + j * TP: dslot * ND * TP + (j + 1) * TP]
            dtus = dtu_s[:, dslot * ND * TP + j * TP: dslot * ND * TP + (j + 1) * TP]
            D_("vector", lambda e, dtus=dtus:
               e.tensor_tensor(ball[:].rearrange("p (n t) -> p n t", n=NST),
                               dtus[:, None].broadcast_to((128, NST, TP)),
                               repB[:].rearrange("p (n t) -> p n t", n=NST), ALU.mult),
               waits=[(pg, mark[("dt", c, j)]), (rep, mark[("rep", c)])], inc=pv)
            mark[("p1", c, j)] = pv.n
            for n in range(NST):
                g = (c * ND + j) * NST + n
                aslot = a_s[:, (n % 2) * TP:(n % 2 + 1) * TP]
                D_("scalar", lambda e, dts=dts, j=j, n=n, aslot=aslot:
                   e.activation(aslot, dts, AF.Exp, scale=asc[:, j * NST + n: j * NST + n + 1]),
                   waits=[(pg, mark[("dt", c, j)]), (pv, ttsm.get(g - 2, 0))], inc=ps_)
                am = ps_.n
                D_("vector", lambda e, n=n, j=j, aslot=aslot:
                   e.tensor_tensor_scan(hall[:, n * TP:(n + 1) * TP], aslot, ball[:, n * TP:(n + 1) * TP],
                                        states[:, n * ND + j: n * ND + j + 1], ALU.mult, ALU.add),
                   waits=[(ps_, am)], inc=pv)
                ttsm[g] = pv.n
            if j == ND - 1:
                mark[("aexp", c)] = ps_.n
            D_("vector", lambda e, j=j:
               e.tensor_copy(states[:, j: j + (NST - 1) * ND + 1: ND], hall[:, TP - 1:: TP]),
               waits=[pv.now()], inc=pv)
            D_("vector", lambda e:
               e.tensor_tensor(zall[:].rearrange("p (n t) -> p n t", n=NST),
                               hall[:].rearrange("p (n t) -> p n t", n=NST),
                               repC[:].rearrange("p (n t) -> p n t", n=NST), ALU.mult),
               inc=pv)
            if j == ND - 1:
                mark[("repfree", c)] = pv.n
            D_("vector", lambda e:
               e.tensor_reduce(zred[:], zall[:].rearrange("p (n t) -> p t n", n=NST), AX.X, ALU.add),
               inc=pv)
            D_("vector", lambda e, j=j:
               e.scalar_tensor_tensor(y_t[:], uh[:, dslot * ND * TP + j * TP: dslot * ND * TP + (j + 1) * TP],
                                      dsk_s[:, j:j + 1], zred[:], ALU.mult, ALU.add),
               waits=[(ps_, mark[("uh", c, j)])], inc=pv)
            mark[("yD", c, j)] = pv.n
            D_("vector", lambda e, j=j, dslot=dslot:
               e.tensor_tensor(yg_s[:, j * TP:(j + 1) * TP], y_t[:],
                               szs[:, dslot * ND * TP + j * TP: dslot * ND * TP + (j + 1) * TP], ALU.mult),
               waits=[(pt_, mark.get(("wout_mm", c - 1), 0)), (ps_, mark[("sz", c)])], inc=pv)
            mark[("yg", c, j)] = pv.n

        def emit_out(c):
            for dm in range(ND):
                bname = ("pA", "pB")[dm % 2]
                pp = pA if bname == "pA" else pB
                for k in range(ND):
                    w = [(pv, mark[("yg", c, ND - 1)]), bk_acq(bname)] if k == 0 else ()
                    D_("tensor", lambda e, pp=pp, dm=dm, k=k:
                       e.matmul(pp[:, 0:TP], woutT[k][:, dm * 128:(dm + 1) * 128],
                                yg_s[:, k * TP:(k + 1) * TP], start=(k == 0), stop=(k == ND - 1)),
                       waits=w, inc=pt_ if k == ND - 1 else None)
                mmk = pt_.n
                if dm == ND - 1:
                    mark[("wout_mm", c)] = pt_.n
                ofs = ofm[:, (dm % 2) * TP:(dm % 2 + 1) * TP]
                w_ofm = mark.get(("otT", c, dm - 2), mark.get(("otT", c - 1, ND + dm - 2), 0))
                D_("scalar", lambda e, pp=pp, ofs=ofs:
                   e.copy(ofs, pp[:, 0:TP]),
                   waits=[(pt_, mmk), (pt_, w_ofm)], inc=ps_)
                bk_rel(bname, ps_)
                ofm_m = ps_.n
                for tau in range(2):
                    bnT = pF_next()
                    pT = pA if bnT == "pA" else pB
                    D_("tensor", lambda e, pT=pT, ofs=ofs, tau=tau:
                       e.transpose(pT[:, 0:128], ofs[:, tau * 128:(tau + 1) * 128], eye_f[:]),
                       waits=[bk_acq(bnT), (ps_, ofm_m)], inc=pt_)
                    D_("scalar", lambda e, pT=pT, tau=tau, dm=dm:
                       e.copy(otm[:, tau * D + dm * 128: tau * D + (dm + 1) * 128], pT[:, 0:128]),
                       waits=[pt_.now(), (po, 32 * c)], inc=ps_)
                    bk_rel(bnT, ps_)
                    mark[("otm_e", c, dm, tau)] = ps_.n
                mark[("otT", c, dm)] = pt_.n
            for tau in range(2):
                D_("sync", lambda e, c=c, tau=tau:
                   dma(e, part[c * TP + tau * 128: c * TP + (tau + 1) * 128, :],
                       otm[:, tau * D:(tau + 1) * D]),
                   waits=[(ps_, mark[("otm_e", c, ND - 1, tau)])], inc=po, incn=16)

        x_dma(0)
        x_dma(1)
        for grp in chain(0):
            for t in grp:
                t()
        for c in range(NC):
            x_dma(c + 2)
            th_next = chain(c + 1) if c + 1 < NC else None
            for j in range(ND):
                scan_dtile(c, j)
                if th_next is not None:
                    for t in th_next[j]:
                        t()
            if th_next is not None:
                for t in th_next[8]:
                    t()
            emit_out(c)

        if DBG[0]:
            dbg_specs = [("d_xnT", xnT, None), ("d_ub", ub_s, None), ("d_uh", uh, ND * TP),
                         ("d_dt", dt_s, ND * TP), ("d_dtu", dtu_s, ND * TP), ("d_szs", szs, ND * TP),
                         ("d_repB", repB, None), ("d_repC", repC, None),
                         ("d_hall", hall, None), ("d_yg", yg_s, None), ("d_states", states, None),
                         ("d_ball", ball, None), ("d_asc", asc, None), ("d_wb", wb_s, None),
                         ("d_xa", xa, 2 * D)]
            for nm, tile, w_ in dbg_specs:
                w_ = w_ or tile.shape[1]
                od = nc.dram_tensor(nm, [128, w_], tile.dtype, kind="ExternalOutput").ap()
                D_("sync", lambda e, od=od, tile=tile, w_=w_: dma(e, od[:, :], tile[0:128, 0:w_]),
                   waits=[(po, po.n)], inc=po, incn=16)
        D_("sync", lambda e: e.nop(), waits=[(po, po.n)])
        P.simulate()
        block = cm.enter_context(nc.Block())
        P.attach(block)
    return nc


DBG = [False]



def build_l2(TOK=2048, TC=128):
    NCH = TOK // TC
    NT = TC // 128
    nc = bass.Bass("TRN2", target_bir_lowering=False, debug=False, detect_race_conditions=False)
    dr = lambda n, s, dt, k: nc.dram_tensor(n, s, dt, kind=k).ap()
    xf = dr("xf", [TOK, D], F32, "ExternalInput")
    pr = [dr(f"p{i}", [TOK, D], F32, "ExternalInput") for i in range(4)]
    w1 = dr("w1", [HF, D], F32, "ExternalInput")
    b1 = dr("b1", [HF, 1], F32, "ExternalInput")
    w2 = dr("w2", [D, HF], F32, "ExternalInput")
    b2 = dr("b2", [D, 1], F32, "ExternalInput")
    eyeb = dr("eyeb", [128, 128], BF16, "ExternalInput")
    out = dr("o", [TOK, D], F32, "ExternalOutput")

    NH = HF // 128
    NK = D // 128

    cm = contextlib.ExitStack()
    with cm:
        bw1T = [_sb(nc, cm, f"bw1T{k}", [128, HF], BF16) for k in range(NK)]
        bw2T = [_sb(nc, cm, f"bw2T{k}", [128, D], BF16) for k in range(NH)]
        al1 = _sb(nc, cm, "al1", [128, NH], F32)
        al2p = _sb(nc, cm, "al2p", [128, 4 * NK], F32)
        al2 = _sb(nc, cm, "al2", [128, NK], F32)
        al2tA = _sb(nc, cm, "al2tA", [128, NK], F32)
        al2tB = _sb(nc, cm, "al2tB", [128, NK], F32)
        al2r = _sb(nc, cm, "al2r", [128, NK], F32)
        b1_s = _sb(nc, cm, "b1_s", [128, NH], F32)
        b2_s = _sb(nc, cm, "b2_s", [128, NK], F32)
        eye_b = _sb(nc, cm, "eye_b", [128, 128], BF16)
        wst = _sb(nc, cm, "wst", [128, 2 * D], F32)
        wsb = _sb(nc, cm, "wsb", [128, 2 * D], BF16)
        xs = _sb(nc, cm, "xs", [128, 2 * NT * D], F32)
        racc = _sb(nc, cm, "racc", [128, 2 * NT * D], F32)
        r1 = _sb(nc, cm, "r1", [128, 2 * NT * D], F32)
        sqg = _sb(nc, cm, "sqg", [128, D], F32)
        ln_s = _sb(nc, cm, "ln_s", [128, 2 * NT], F32)
        ln_q = _sb(nc, cm, "ln_q", [128, 2 * NT], F32)
        ln_m = _sb(nc, cm, "ln_m", [128, 2 * NT], F32)
        ln_v = _sb(nc, cm, "ln_v", [128, 2 * NT], F32)
        ln_t = _sb(nc, cm, "ln_t", [128, 2 * NT], F32)
        q1_s = _sb(nc, cm, "q1_s", [128, 2 * NT], F32)
        s1a = _sb(nc, cm, "s1a", [128, 2 * NT], F32)
        s1b = _sb(nc, cm, "s1b", [128, 2 * NT], F32)
        xn2 = _sb(nc, cm, "xn2", [128, NT * D], BF16)
        xnT2 = _sb(nc, cm, "xnT2", [128, NK * TC], BF16)
        g_all = _sb(nc, cm, "g_all", [128, NH * TC], BF16)
        ffm = _sb(nc, cm, "ffm", [128, 2 * TC], BF16)
        otm = _sb(nc, cm, "otm", [128, NT * D], F32)
        zzpad = _sb(nc, cm, "zzpad", [128, 96], F32)
        pA = _ps(nc, cm, "pA")
        pB = _ps(nc, cm, "pB")
        pTa = _ps(nc, cm, "pTa", BF16)
        pTb = _ps(nc, cm, "pTb", BF16)

        P = Plan()
        S = {k: Sem(nc, cm, k) for k in ("dx0", "dx1", "dwt", "dac", "po", "ps", "pv", "pt", "pg")}
        dxq = (S["dx0"], S["dx1"])
        dwt, dac, po = (S[k] for k in ("dwt", "dac", "po"))
        ps_, pv, pt_, pg = (S[k] for k in ("ps", "pv", "pt", "pg"))
        mark = {}
        banks = {b: (ps_, 0) for b in ("pA", "pB", "pTa", "pTb")}

        def bk_acq(b):
            s, v = banks[b]
            return (s, v)

        def bk_rel(b, sem):
            banks[b] = (sem, sem.n)

        tcyc = [0]

        def D_(eng, fn, waits=(), inc=None, incn=1):
            P.do(eng, fn, waits, inc, incn)

        def dma(e, dst, src):
            return e.dma_start(out=dst, in_=src)

        D_("sync", lambda e: dma(e, eye_b[:], eyeb[:]), inc=dwt, incn=16)
        for h in range(NH):
            D_("sync", lambda e, h=h: dma(e, b1_s[:, h:h + 1], b1[h * 128:(h + 1) * 128, :]), inc=dwt, incn=16)
        for k in range(NK):
            D_("sync", lambda e, k=k: dma(e, b2_s[:, k:k + 1], b2[k * 128:(k + 1) * 128, :]), inc=dwt, incn=16)
        dwt_c = dwt.now()

        def bin_block(src, r0, col0, alpha_dst, dst_sel, tag, ti):
            slot = ti % 2
            st_f = wst[:, slot * D: slot * D + D]
            st_b = wsb[:, slot * D: slot * D + D]
            D_("sync", lambda e, st_f=st_f: dma(e, st_f, src[r0:r0 + 128, col0:col0 + D]),
               waits=[(ps_, mark.get((tag, ti - 2), 0)), (pv, mark.get((tag, "v", ti - 2), 0)),
                      (dwt, dwt.n)], inc=dwt, incn=16)
            D_("vector", lambda e, st_f=st_f, alpha_dst=alpha_dst:
               e.tensor_reduce(alpha_dst, st_f, AX.X, ALU.add, apply_absolute_value=True),
               waits=[dwt.now()], inc=pv)
            mark[(tag, "v", ti)] = pv.n
            D_("scalar", lambda e, st_f=st_f, st_b=st_b: e.activation(st_b, st_f, AF.Sign),
               waits=[dwt.now(), pv.now()], inc=ps_)
            sg = ps_.now()
            for k in range(NK):
                bname = ("pTa", "pTb")[tcyc[0] % 2]
                tcyc[0] += 1
                pT = pTa if bname == "pTa" else pTb
                dst, dcol = dst_sel(k)
                D_("tensor", lambda e, pT=pT, st_b=st_b, k=k:
                   e.transpose(pT[:, 0:128], st_b[:, k * 128:(k + 1) * 128], eye_b[:]),
                   waits=[bk_acq(bname), sg, dwt_c], inc=pt_)
                D_("scalar", lambda e, pT=pT, dst=dst, dcol=dcol:
                   e.copy(dst[:, dcol:dcol + 128], pT[:, 0:128]),
                   waits=[pt_.now()], inc=ps_)
                bk_rel(bname, ps_)
            mark[(tag, ti)] = ps_.n

        ti = 0
        for r in range(NH):
            bin_block(w1, r * 128, 0, al1[:, r:r + 1], lambda k, r=r: (bw1T[k], r * 128), "bb", ti)
            ti += 1
        for r in range(NK):
            for bcol in range(4):
                bin_block(w2, r * 128, bcol * D, al2p[:, bcol * NK + r:bcol * NK + r + 1],
                          lambda k, r=r, bcol=bcol: (bw2T[bcol * 8 + k], r * 128), "bb", ti)
                ti += 1
        D_("vector", lambda e: e.tensor_copy(al2tA[:, 0:NK], al1[:, 0:NK]), inc=pv)
        D_("vector", lambda e: e.tensor_scalar(al1[:], al1[:], 1.0 / D, None, ALU.mult), waits=[pv.now()], inc=pv)
        D_("vector", lambda e: e.tensor_tensor(al2tA[:], al2p[:, 0:NK], al2p[:, NK:2 * NK], ALU.add), waits=[pv.now()], inc=pv)
        D_("vector", lambda e: e.tensor_tensor(al2tB[:], al2p[:, 2 * NK:3 * NK], al2p[:, 3 * NK:4 * NK], ALU.add), waits=[pv.now()], inc=pv)
        D_("vector", lambda e: e.tensor_tensor(al2r[:], al2tA[:], al2tB[:], ALU.add), waits=[pv.now()], inc=pv)
        D_("vector", lambda e: e.tensor_scalar(al2[:], al2r[:], 1.0 / HF, None, ALU.mult), waits=[pv.now()], inc=pv)
        P.base = [dwt_c, (pv, pv.n)]

        def xp_dma(c):
            if c >= NCH:
                return
            slot = c % 2
            w = [(pv, mark.get(("r1done", c - 2), 0)), (po, max(0, (c - 1)) * NT * 16)]
            dxs_ = dxq[c % 2]
            D_("sync", lambda e, c=c, slot=slot:
               dma(e, xs[:, slot * NT * D:(slot + 1) * NT * D].rearrange("p (t d) -> p t d", d=D),
                   xf[c * TC:(c + 1) * TC, :].rearrange("(t p) d -> p t d", p=128)),
               waits=w, inc=dxs_, incn=16)
            mark[("xdma", c)] = (dxs_, dxs_.n)
            rs = racc[:, slot * NT * D:(slot + 1) * NT * D].rearrange("p (t d) -> p t d", d=D)
            for i in range(4):
                D_("gpsimd", lambda e, i=i, c=c, rs=rs:
                   e.dma_start(out=rs, in_=pr[i][c * TC:(c + 1) * TC, :].rearrange("(t p) d -> p t d", p=128),
                               accum_op=(ALU.bypass if i == 0 else ALU.add)),
                   waits=w + [(dac, dac.n)], inc=dac, incn=16)
            mark[("pacc", c)] = dac.n

        xp_dma(0)
        xp_dma(1)
        for c in range(NCH):
            slot = c % 2
            xc = xs[:, slot * NT * D:(slot + 1) * NT * D]
            rc = racc[:, slot * NT * D:(slot + 1) * NT * D]
            r1c = r1[:, slot * NT * D:(slot + 1) * NT * D]
            cs = slice(slot * NT, slot * NT + NT)
            for tau in range(NT):
                xt = xc[:, tau * D:(tau + 1) * D]
                col = slot * NT + tau
                D_("scalar", lambda e, xt=xt, col=col:
                   e.activation(sqg[:], xt, AF.Square, accum_out=ln_q[:, col:col + 1]),
                   waits=[mark[("xdma", c)], ps_.now(), pv.now()], inc=ps_)
                D_("vector", lambda e, xt=xt, col=col:
                   e.tensor_reduce(ln_s[:, col:col + 1], xt, AX.X, ALU.add),
                   waits=[mark[("xdma", c)]], inc=pv)
            D_("vector", lambda e, cs=cs: e.tensor_scalar(ln_m[:, cs], ln_s[:, cs], 1.0 / D, None, ALU.mult),
               waits=[ps_.now(), pv.now()], inc=pv)
            D_("vector", lambda e, cs=cs: e.tensor_scalar(ln_v[:, cs], ln_q[:, cs], 1.0 / D, None, ALU.mult), waits=[pv.now()], inc=pv)
            D_("vector", lambda e, cs=cs: e.tensor_tensor(ln_t[:, cs], ln_m[:, cs], ln_m[:, cs], ALU.mult), waits=[pv.now()], inc=pv)
            D_("vector", lambda e, cs=cs: e.tensor_tensor(ln_v[:, cs], ln_v[:, cs], ln_t[:, cs], ALU.subtract), waits=[pv.now()], inc=pv)
            D_("vector", lambda e, cs=cs: e.tensor_scalar(ln_t[:, cs], ln_v[:, cs], 1.0, EPS, ALU.mult, ALU.add), waits=[pv.now()], inc=pv)
            D_("scalar", lambda e, cs=cs: e.activation(ln_v[:, cs], ln_t[:, cs], AF.Ln), waits=[pv.now()], inc=ps_)
            D_("scalar", lambda e, cs=cs: e.activation(q1_s[:, cs], ln_v[:, cs], AF.Exp, scale=-0.5), inc=ps_)
            D_("vector", lambda e, cs=cs: e.tensor_scalar(s1a[:, cs], q1_s[:, cs], 1.0, None, ALU.add),
               waits=[ps_.now(), pv.now()], inc=pv)
            D_("vector", lambda e, cs=cs: e.tensor_tensor(s1b[:, cs], ln_m[:, cs], q1_s[:, cs], ALU.mult), waits=[pv.now()], inc=pv)
            for tau in range(NT):
                col = slot * NT + tau
                D_("vector", lambda e, tau=tau, col=col, xc=xc:
                   e.tensor_scalar(xc[:, tau * D:(tau + 1) * D], xc[:, tau * D:(tau + 1) * D],
                                   s1a[:, col:col + 1], s1b[:, col:col + 1], ALU.mult, ALU.subtract),
                   inc=pv)
            D_("vector", lambda e, r1c=r1c, xc=xc, rc=rc: e.tensor_tensor(r1c, xc, rc, ALU.add),
               waits=[(dac, mark[("pacc", c)])], inc=pv)
            mark[("r1", c)] = pv.n
            for tau in range(NT):
                col = slot * NT + tau
                D_("scalar", lambda e, tau=tau, col=col, r1c=r1c:
                   e.activation(sqg[:], r1c[:, tau * D:(tau + 1) * D], AF.Square, accum_out=ln_q[:, col:col + 1]),
                   waits=[(pv, mark[("r1", c)]), ps_.now(), pv.now()], inc=ps_)
                D_("vector", lambda e, tau=tau, col=col, r1c=r1c:
                   e.tensor_reduce(ln_s[:, col:col + 1], r1c[:, tau * D:(tau + 1) * D], AX.X, ALU.add),
                   inc=pv)
            D_("vector", lambda e, cs=cs: e.tensor_scalar(ln_m[:, cs], ln_s[:, cs], 1.0 / D, None, ALU.mult),
               waits=[ps_.now(), pv.now()], inc=pv)
            D_("vector", lambda e, cs=cs: e.tensor_scalar(ln_v[:, cs], ln_q[:, cs], 1.0 / D, None, ALU.mult), waits=[pv.now()], inc=pv)
            D_("vector", lambda e, cs=cs: e.tensor_tensor(ln_t[:, cs], ln_m[:, cs], ln_m[:, cs], ALU.mult), waits=[pv.now()], inc=pv)
            D_("vector", lambda e, cs=cs: e.tensor_tensor(ln_v[:, cs], ln_v[:, cs], ln_t[:, cs], ALU.subtract), waits=[pv.now()], inc=pv)
            D_("vector", lambda e, cs=cs: e.tensor_scalar(ln_t[:, cs], ln_v[:, cs], 1.0, EPS, ALU.mult, ALU.add), waits=[pv.now()], inc=pv)
            D_("scalar", lambda e, cs=cs: e.activation(ln_v[:, cs], ln_t[:, cs], AF.Ln), waits=[pv.now()], inc=ps_)
            D_("scalar", lambda e, cs=cs: e.activation(q1_s[:, cs], ln_v[:, cs], AF.Exp, scale=-0.5), inc=ps_)
            for tau in range(NT):
                col = slot * NT + tau
                D_("vector", lambda e, tau=tau, col=col, r1c=r1c:
                   e.tensor_scalar(xn2[:, tau * D:(tau + 1) * D], r1c[:, tau * D:(tau + 1) * D],
                                   ln_m[:, col:col + 1], q1_s[:, col:col + 1], ALU.subtract, ALU.mult),
                   waits=[ps_.now(), (ps_, mark.get(("xnTe", c - 1), 0))], inc=pv)
            mark[("xn2", c)] = pv.n
            for tau in range(NT):
                for k in range(NK):
                    bname = ("pTa", "pTb")[tcyc[0] % 2]
                    tcyc[0] += 1
                    pT = pTa if bname == "pTa" else pTb
                    D_("tensor", lambda e, pT=pT, tau=tau, k=k:
                       e.transpose(pT[:, 0:128], xn2[:, tau * D + k * 128: tau * D + (k + 1) * 128], eye_b[:]),
                       waits=[bk_acq(bname), (pv, mark[("xn2", c)])], inc=pt_)
                    D_("scalar", lambda e, pT=pT, tau=tau, k=k:
                       e.copy(xnT2[:, k * TC + tau * 128: k * TC + tau * 128 + 128], pT[:, 0:128]),
                       waits=[pt_.now()], inc=ps_)
                    bk_rel(bname, ps_)
            mark[("xnTe", c)] = ps_.n
            for h in range(NH):
                bname = ("pA", "pB")[h % 2]
                pp = pA if bname == "pA" else pB
                for k in range(NK):
                    w = [(ps_, mark[("xnTe", c)]), bk_acq(bname)] if k == 0 else ()
                    D_("tensor", lambda e, pp=pp, h=h, k=k:
                       e.matmul(pp[:, 0:TC], bw1T[k][:, h * 128:(h + 1) * 128],
                                xnT2[:, k * TC:(k + 1) * TC], start=(k == 0), stop=(k == NK - 1)),
                       waits=w, inc=pt_ if k == NK - 1 else None)
                mmk = pt_.n
                if h == NH - 1:
                    mark[("w1_mm", c)] = pt_.n
                D_("scalar", lambda e, pp=pp, h=h:
                   e.activation(g_all[:, h * TC:(h + 1) * TC], pp[:, 0:TC], AF.Gelu,
                                bias=b1_s[:, h:h + 1], scale=al1[:, h:h + 1]),
                   waits=[(pt_, mmk), (pt_, mark.get(("w2_mm", c - 1), 0))], inc=ps_)
                bk_rel(bname, ps_)
            mark[("gall", c)] = ps_.n
            for dm in range(NK):
                bname = ("pA", "pB")[dm % 2]
                pp = pA if bname == "pA" else pB
                for k in range(NH):
                    w = [(ps_, mark[("gall", c)]), bk_acq(bname)] if k == 0 else ()
                    D_("tensor", lambda e, pp=pp, dm=dm, k=k:
                       e.matmul(pp[:, 0:TC], bw2T[k][:, dm * 128:(dm + 1) * 128],
                                g_all[:, k * TC:(k + 1) * TC], start=(k == 0), stop=(k == NH - 1)),
                       waits=w, inc=pt_ if k == NH - 1 else None)
                mmk = pt_.n
                if dm == NK - 1:
                    mark[("w2_mm", c)] = pt_.n
                ffs = ffm[:, (dm % 2) * TC:(dm % 2 + 1) * TC]
                w_ffm = mark.get(("oT", c, dm - 2), mark.get(("oT", c - 1, NK + dm - 2), 0))
                D_("scalar", lambda e, pp=pp, ffs=ffs, dm=dm:
                   e.activation(ffs, pp[:, 0:TC], AF.Identity, bias=b2_s[:, dm:dm + 1], scale=al2[:, dm:dm + 1]),
                   waits=[(pt_, mmk), (pt_, w_ffm)], inc=ps_)
                bk_rel(bname, ps_)
                ffm_m = ps_.n
                for tau in range(NT):
                    bnT = ("pTa", "pTb")[tcyc[0] % 2]
                    tcyc[0] += 1
                    pT = pTa if bnT == "pTa" else pTb
                    D_("tensor", lambda e, pT=pT, ffs=ffs, tau=tau:
                       e.transpose(pT[:, 0:128], ffs[:, tau * 128:(tau + 1) * 128], eye_b[:]),
                       waits=[bk_acq(bnT), (ps_, ffm_m)], inc=pt_)
                    D_("scalar", lambda e, pT=pT, tau=tau, dm=dm:
                       e.copy(otm[:, tau * D + dm * 128: tau * D + (dm + 1) * 128], pT[:, 0:128]),
                       waits=[pt_.now(), (po, NT * 16 * c)], inc=ps_)
                    bk_rel(bnT, ps_)
                    mark[("ote", c, dm, tau)] = ps_.n
                mark[("oT", c, dm)] = pt_.n
            for tau in range(NT):
                D_("vector", lambda e, tau=tau, r1c=r1c, rc=rc:
                   e.tensor_tensor(rc[:, tau * D:(tau + 1) * D], otm[:, tau * D:(tau + 1) * D],
                                   r1c[:, tau * D:(tau + 1) * D], ALU.add),
                   waits=[(ps_, mark[("ote", c, NK - 1, tau)])], inc=pv)
                mark[("oadd", c, tau)] = pv.n
            mark[("r1done", c)] = pv.n
            for tau in range(NT):
                D_("sync", lambda e, c=c, tau=tau, rc=rc:
                   dma(e, out[c * TC + tau * 128: c * TC + (tau + 1) * 128, :], rc[:, tau * D:(tau + 1) * D]),
                   waits=[(pv, mark[("oadd", c, tau)])], inc=po, incn=16)
            xp_dma(c + 2)

        if DBG[0]:
            dbg2 = [("e_al1", al1), ("e_al2", al2), ("e_al2p", al2p), ("e_al2r", al2r), ("e_al2tA", al2tA), ("e_al2tB", al2tB), ("e_bw1T0", bw1T[0]), ("e_bw2T0", bw2T[0]),
                    ("e_racc", racc), ("e_r1", r1), ("e_xn2", xn2), ("e_xnT2", xnT2),
                    ("e_gall", g_all), ("e_otm", otm), ("e_q1", q1_s), ("e_lnm", ln_m)]
            for nm, tile in dbg2:
                od = nc.dram_tensor(nm, [128, tile.shape[1]], tile.dtype, kind="ExternalOutput").ap()
                D_("sync", lambda e, od=od, tile=tile: dma(e, od[:, :], tile[0:128, :]),
                   waits=[(po, po.n)], inc=po, incn=16)
        D_("sync", lambda e: e.nop(), waits=[(po, po.n)])
        P.simulate()
        block = cm.enter_context(nc.Block())
        P.attach(block)
    return nc



def _l1_in_maps(inp, x, eyef, eyeb):
    f = lambda k: np.ascontiguousarray(np.asarray(inp[k], dtype=np.float32))
    in_maps = []
    for c in range(8):
        b, dire, dh = c >> 2, (c >> 1) & 1, c & 1
        pfx = "f" if dire == 0 else "b"
        xdir = x[b] if dire == 0 else x[b, ::-1]
        winf = f(f"L0_attn_{pfx}_Win")
        u_rows = winf[0:E]
        z_rows = winf[E + dh * EH: E + (dh + 1) * EH]
        cwm = f(f"L0_attn_{pfx}_convw").reshape(E, 4)
        cbm = f(f"L0_attn_{pfx}_convb").reshape(E, 1)
        wxm = f(f"L0_attn_{pfx}_Wx")
        if dh == 1:
            u_rows = np.concatenate([u_rows[EH:], u_rows[0:EH]], axis=0)
            cwm = np.concatenate([cwm[EH:], cwm[0:EH]], axis=0)
            cbm = np.concatenate([cbm[EH:], cbm[0:EH]], axis=0)
            wxm = np.concatenate([wxm[:, EH:], wxm[:, 0:EH]], axis=1)
        m = {
            "xd": np.ascontiguousarray(xdir),
            "win": np.ascontiguousarray(np.concatenate([u_rows, z_rows], axis=0)),
            "convw": np.ascontiguousarray(cwm),
            "convb": np.ascontiguousarray(cbm),
            "wx": np.ascontiguousarray(wxm),
            "wdt": np.ascontiguousarray(f(f"L0_attn_{pfx}_Wdt")[dh * EH:(dh + 1) * EH]),
            "bdt": np.ascontiguousarray(f(f"L0_attn_{pfx}_bdt").reshape(E, 1)[dh * EH:(dh + 1) * EH]),
            "alog": np.ascontiguousarray(f(f"L0_attn_{pfx}_Alog")[dh * EH:(dh + 1) * EH]),
            "dsk": np.ascontiguousarray(f(f"L0_attn_{pfx}_D").reshape(E, 1)[dh * EH:(dh + 1) * EH]),
            "wout": np.ascontiguousarray(f(f"L0_attn_{pfx}_Wout")[:, dh * EH:(dh + 1) * EH]),
            "gvec": f("L0_attn_norm_g").reshape(D, 1),
            "bvec": f("L0_attn_norm_b").reshape(D, 1),
            "eyef": eyef, "eyeb": eyeb,
        }
        in_maps.append(m)
    return in_maps


def kernel(**inp):
    f = lambda k: np.ascontiguousarray(np.asarray(inp[k], dtype=np.float32))
    x = f("L0_attn_x")
    L = x.shape[1]
    eyef = np.eye(128, dtype=np.float32)
    eyeb = np.eye(128).astype(BF16_NP)

    nc1 = build_l1(L)
    in_maps = _l1_in_maps(inp, x, eyef, eyeb)
    res1 = run_bass_kernel_spmd(nc1, in_maps, core_ids=list(range(8)))
    parts = []
    for c in range(8):
        p = np.asarray(res1.results[c]["part"], dtype=np.float32)
        if (c >> 1) & 1:
            p = p[::-1]
        parts.append(np.ascontiguousarray(p))

    nc2 = build_l2()
    in_maps2 = []
    for k in range(8):
        bk = (k * 2048) // L
        lo = k * 2048 - bk * L
        sl = slice(lo, lo + 2048)
        m = {
            "xf": np.ascontiguousarray(x[bk][sl]),
            "w1": f("L0_ffn_w1"),
            "b1": f("L0_ffn_b1").reshape(HF, 1),
            "w2": f("L0_ffn_w2"),
            "b2": f("L0_ffn_b2").reshape(D, 1),
            "eyeb": eyeb,
        }
        for i in range(4):
            m[f"p{i}"] = parts[bk * 4 + i][sl]
        in_maps2.append(m)
    res2 = run_bass_kernel_spmd(nc2, in_maps2, core_ids=list(range(8)))
    outs = [np.asarray(res2.results[k]["o"], dtype=np.float32) for k in range(8)]
    return np.concatenate(outs, axis=0).reshape(2, L, D)


# version 06
# speedup vs baseline: 1.7897x; 1.7897x over previous
import contextlib
import numpy as np
import ml_dtypes
import concourse.bass as bass
import concourse.mybir as mybir
from concourse.bass_utils import run_bass_kernel_spmd

F32 = mybir.dt.float32
BF16 = mybir.dt.bfloat16
AF = mybir.ActivationFunctionType
ALU = mybir.AluOpType
AX = mybir.AxisListType

D = 1024
E = 2048
EH = 1024
NST = 16
RNK = 64
HF = 4096
TP = 256
EPS = 1e-5
ENGINES = ("sync", "scalar", "vector", "tensor", "gpsimd")

BF16_NP = ml_dtypes.bfloat16


class Sem:
    def __init__(self, nc, cm, name):
        self.h = cm.enter_context(nc.semaphore(name))
        self.name = name
        self.n = 0

    def now(self):
        return (self, self.n)


class Plan:
    def __init__(self):
        self.ops = {e: [] for e in ENGINES}
        self.base = []

    def do(self, eng, fn, waits=(), inc=None, incn=1):
        waits = tuple((s, v) for (s, v) in tuple(waits) + tuple(self.base) if v > 0)
        if inc is not None:
            inc.n += incn
        self.ops[eng].append((waits, fn, (inc, incn) if inc is not None else None))

    def simulate(self):
        counts = {}
        ptr = {e: 0 for e in ENGINES}
        total = sum(len(v) for v in self.ops.values())
        done = 0
        progress = True
        while progress:
            progress = False
            for e in ENGINES:
                while ptr[e] < len(self.ops[e]):
                    waits, _, inc = self.ops[e][ptr[e]]
                    if any(counts.get(s.name, 0) < v for s, v in waits):
                        break
                    if inc is not None:
                        s, n = inc
                        counts[s.name] = counts.get(s.name, 0) + n
                    ptr[e] += 1
                    done += 1
                    progress = True
        if done != total:
            stuck = {e: (ptr[e], len(self.ops[e])) for e in ENGINES}
            detail = {}
            for e in ENGINES:
                if ptr[e] < len(self.ops[e]):
                    waits, _, _ = self.ops[e][ptr[e]]
                    detail[e] = [(s.name, v, counts.get(s.name, 0)) for s, v in waits]
            raise RuntimeError(f"plan deadlock: {stuck} waiting {detail}")

    def attach(self, block):
        def mk(name):
            def body(e):
                last = {}
                for waits, fn, inc in self.ops[name]:
                    for s, v in waits:
                        if v > last.get(s.name, -1):
                            e.wait_ge(s.h, v)
                            last[s.name] = v
                    ins = fn(e)
                    if inc is not None:
                        s, n = inc
                        ins.then_inc(s.h, n)
            return body
        block.sync(mk("sync"))
        block.scalar(mk("scalar"))
        block.vector(mk("vector"))
        block.tensor(mk("tensor"))
        block.gpsimd(mk("gpsimd"))


def _sb(nc, cm, name, shape, dt):
    return cm.enter_context(nc.sbuf_tensor(name, shape, dt))


def _ps(nc, cm, name, dt=F32):
    n = 512 if dt == F32 else 1024
    return cm.enter_context(nc.psum_tensor(name, [128, n], dt))



def build_l1(L=8192):
    NC = L // TP
    nc = bass.Bass("TRN2", target_bir_lowering=False, debug=False, detect_race_conditions=False)
    dr = lambda n, s, dt, k: nc.dram_tensor(n, s, dt, kind=k).ap()
    xd = dr("xd", [L, D], F32, "ExternalInput")
    win = dr("win", [E + EH, D], F32, "ExternalInput")
    convw = dr("convw", [E, 4], F32, "ExternalInput")
    convb = dr("convb", [E, 1], F32, "ExternalInput")
    wx = dr("wx", [RNK + 2 * NST, E], F32, "ExternalInput")
    wdt = dr("wdt", [EH, RNK], F32, "ExternalInput")
    bdt = dr("bdt", [EH, 1], F32, "ExternalInput")
    alog = dr("alog", [EH, NST], F32, "ExternalInput")
    dsk = dr("dsk", [EH, 1], F32, "ExternalInput")
    wout = dr("wout", [D, EH], F32, "ExternalInput")
    gvec = dr("gvec", [D, 1], F32, "ExternalInput")
    bvec = dr("bvec", [D, 1], F32, "ExternalInput")
    eyef = dr("eyef", [128, 128], F32, "ExternalInput")
    eyeb = dr("eyeb", [128, 128], BF16, "ExternalInput")
    part = dr("part", [L, D], F32, "ExternalOutput")
    bscr = dr("bscr", [2 * NST, L], BF16, "Internal")

    NU = E // 128
    NE = (E + EH) // 128
    ND = EH // 128
    NK = D // 128

    cm = contextlib.ExitStack()
    with cm:
        winT = [_sb(nc, cm, f"winT{k}", [128, NE * 128], BF16) for k in range(NK)]
        woutT = [_sb(nc, cm, f"woutT{k}", [128, D], BF16) for k in range(ND)]
        wxT = _sb(nc, cm, "wxT", [128, NU * 96], BF16)
        wdtT = _sb(nc, cm, "wdtT", [64, EH], BF16)
        asc = _sb(nc, cm, "asc", [128, ND * NST], F32)
        gv_s = _sb(nc, cm, "gv_s", [128, NK], F32)
        bvb = _sb(nc, cm, "bvb", [128, NK], BF16)
        bv_s = _sb(nc, cm, "bv_s", [128, NK], F32)
        bdt_s = _sb(nc, cm, "bdt_s", [128, ND], F32)
        dsk_s = _sb(nc, cm, "dsk_s", [128, ND], F32)
        cb_s = _sb(nc, cm, "cb_s", [128, NU], F32)
        cw_s = _sb(nc, cm, "cw_s", [128, NU * 4], F32)
        wb_s = _sb(nc, cm, "wb_s", [128, NE], F32)
        eye_f = _sb(nc, cm, "eye_f", [128, 128], F32)
        eye_b = _sb(nc, cm, "eye_b", [128, 128], BF16)
        xa = _sb(nc, cm, "xa", [128, 2 * 2 * D], F32)
        ln_s = _sb(nc, cm, "ln_s", [128, 4], F32)
        ln_q = _sb(nc, cm, "ln_q", [128, 4], F32)
        ln_m = _sb(nc, cm, "ln_m", [128, 4], F32)
        ln_t = _sb(nc, cm, "ln_t", [128, 4], F32)
        ln_v = _sb(nc, cm, "ln_v", [128, 4], F32)
        q_s = _sb(nc, cm, "q_s", [128, 4], F32)
        nmq = _sb(nc, cm, "nmq", [128, 4], F32)
        xnT = _sb(nc, cm, "xnT", [128, NK * TP], BF16)
        uex = _sb(nc, cm, "uex", [128, 2 * (TP + 3)], F32)
        uacc = _sb(nc, cm, "uacc", [128, 2 * TP], F32)
        ctmp = _sb(nc, cm, "ctmp", [128, TP], F32)
        halo = _sb(nc, cm, "halo", [128, NU * 3], F32)
        ub_s = _sb(nc, cm, "ub_s", [128, (NU - ND) * TP], BF16)
        uh = _sb(nc, cm, "uh", [128, 2 * ND * TP], BF16)
        dt_s = _sb(nc, cm, "dt_s", [128, 2 * ND * TP], F32)
        dtu_s = _sb(nc, cm, "dtu_s", [128, 2 * ND * TP], BF16)
        szs = _sb(nc, cm, "szs", [128, 2 * ND * TP], BF16)
        dtr_b = _sb(nc, cm, "dtr_b", [64, TP], BF16)
        bc_sb = _sb(nc, cm, "bc_sb", [128, TP], BF16)
        repB = _sb(nc, cm, "repB", [128, NST * TP], BF16)
        repC = _sb(nc, cm, "repC", [128, NST * TP], BF16)
        ball = _sb(nc, cm, "ball", [128, NST * TP], BF16)
        hall = _sb(nc, cm, "hall", [128, NST * TP], BF16)
        zall = _sb(nc, cm, "zall", [128, NST * TP], BF16)
        zred = _sb(nc, cm, "zred", [128, TP], F32)
        a_s = _sb(nc, cm, "a_s", [128, 4 * TP], F32)
        sp4 = _sb(nc, cm, "sp4", [128, 4 * TP], F32)
        y_t = _sb(nc, cm, "y_t", [128, TP], F32)
        yg_s = _sb(nc, cm, "yg_s", [128, ND * TP], BF16)
        ofm = _sb(nc, cm, "ofm", [128, 2 * TP], F32)
        otm = _sb(nc, cm, "otm", [128, 2 * D], F32)
        states = _sb(nc, cm, "states", [128, 128], F32)
        zzpad = _sb(nc, cm, "zzpad", [128, 96], F32)
        pA = _ps(nc, cm, "pA")
        pB = _ps(nc, cm, "pB")
        pTa = _ps(nc, cm, "pTa", BF16)
        pTb = _ps(nc, cm, "pTb", BF16)
        pWx = _ps(nc, cm, "pWx")
        pM = _ps(nc, cm, "pM")

        P = Plan()
        S = {k: Sem(nc, cm, k) for k in ("dx0", "dx1", "dwt", "dcn", "bcw", "rep", "po",
                                         "ps", "pv", "pt", "pg")}
        dxp = (S["dx0"], S["dx1"])
        dwt, dcn, bcw, rep, po = (S[k] for k in ("dwt", "dcn", "bcw", "rep", "po"))
        ps_, pv, pt_, pg = (S[k] for k in ("ps", "pv", "pt", "pg"))
        mark = {}
        ttsm = {}
        banks = {b: (ps_, 0) for b in ("pA", "pB", "pTa", "pTb", "pWx", "pM")}

        def bk_acq(b):
            s, v = banks[b]
            return (s, v)

        def bk_rel(b, sem):
            banks[b] = (sem, sem.n)

        tcyc = [0]
        tcycf = [0]

        def pT_next():
            t = ("pTa", "pTb")[tcyc[0] % 2]
            tcyc[0] += 1
            return t

        def pF_next():
            t = ("pA", "pB")[tcycf[0] % 2]
            tcycf[0] += 1
            return t

        def D_(eng, fn, waits=(), inc=None, incn=1):
            P.do(eng, fn, waits, inc, incn)

        def dma(e, dst, src):
            return e.dma_start(out=dst, in_=src)

        D_("sync", lambda e: dma(e, eye_f[:], eyef[:]), inc=dcn, incn=16)
        D_("sync", lambda e: dma(e, eye_b[:], eyeb[:]), inc=dcn, incn=16)
        for k in range(NK):
            D_("sync", lambda e, k=k: dma(e, gv_s[:, k:k + 1], gvec[k * 128:(k + 1) * 128, :]), inc=dcn, incn=16)
            D_("sync", lambda e, k=k: dma(e, bv_s[:, k:k + 1], bvec[k * 128:(k + 1) * 128, :]), inc=dcn, incn=16)
        for j in range(ND):
            D_("sync", lambda e, j=j: dma(e, bdt_s[:, j:j + 1], bdt[j * 128:(j + 1) * 128, :]), inc=dcn, incn=16)
            D_("sync", lambda e, j=j: dma(e, dsk_s[:, j:j + 1], dsk[j * 128:(j + 1) * 128, :]), inc=dcn, incn=16)
            D_("sync", lambda e, j=j: dma(e, asc[:, j * NST:(j + 1) * NST], alog[j * 128:(j + 1) * 128, :]), inc=dcn, incn=16)
        for t in range(NU):
            D_("sync", lambda e, t=t: dma(e, cb_s[:, t:t + 1], convb[t * 128:(t + 1) * 128, :]), inc=dcn, incn=16)
            D_("sync", lambda e, t=t: dma(e, cw_s[:, 4 * t:4 * t + 4], convw[t * 128:(t + 1) * 128, :]), inc=dcn, incn=16)
        dcn_all = dcn.now()

        D_("scalar", lambda e: e.activation(sp4[:, 0:ND * NST], asc[:], AF.Exp), waits=[dcn_all], inc=ps_)
        D_("vector", lambda e: e.tensor_scalar(asc[:], sp4[:, 0:ND * NST], -1.0, None, ALU.mult),
           waits=[ps_.now()], inc=pv)
        D_("vector", lambda e: e.tensor_copy(bvb[:], bv_s[:]), waits=[dcn_all], inc=pv)
        D_("vector", lambda e: e.memset(states[:], 0.0), inc=pv)
        D_("vector", lambda e: e.memset(halo[:], 0.0), inc=pv)

        stage_gate = [0]

        def transpose_evac(src_ap, dst_ap, ident, extra_w=()):
            bname = pT_next()
            pT = pTa if bname == "pTa" else pTb
            D_("tensor", lambda e, pT=pT: e.transpose(pT[:, 0:src_ap.shape[0]], src_ap, ident),
               waits=[bk_acq(bname)] + list(extra_w), inc=pt_)
            D_("scalar", lambda e, pT=pT: e.copy(dst_ap, pT[:, 0:src_ap.shape[0]]),
               waits=[pt_.now()], inc=ps_)
            bk_rel(bname, ps_)

        def wprep(dst_sel, src, rows, row_len, tag):
            nkt = row_len // 128
            for r in range(rows):
                slot = r % 2
                st_f = xa[:, slot * 2 * D: slot * 2 * D + row_len]
                st_b = xnT[:, slot * D: slot * D + row_len]
                D_("sync", lambda e, st_f=st_f, r=r: dma(e, st_f, src[r * 128:(r + 1) * 128, :]),
                   waits=[(pt_, stage_gate[0]), (dwt, dwt.n)], inc=dwt, incn=16)
                D_("vector", lambda e, st_b=st_b, st_f=st_f: e.tensor_copy(st_b, st_f),
                   waits=[dwt.now(), (pt_, stage_gate[0])], inc=pv)
                pv_mark = pv.now()
                for k in range(nkt):
                    dst, dcol = dst_sel(r, k)
                    transpose_evac(st_b[:, k * 128:(k + 1) * 128], dst[:, dcol:dcol + 128],
                                   eye_b[:], extra_w=[pv_mark, dcn_all])
                stage_gate[0] = pt_.n

        wprep(lambda r, k: (winT[k], r * 128), win, NE, D, "w_in")
        wprep(lambda r, k: (woutT[k], r * 128), wout, ND, D, "w_out")

        D_("sync", lambda e: dma(e, xa[0:96, 0:E], wx[:, :]),
           waits=[(pt_, stage_gate[0]), (dwt, dwt.n)], inc=dwt, incn=16)
        D_("vector", lambda e: e.tensor_copy(xnT[0:96, 0:E], xa[0:96, 0:E]),
           waits=[dwt.now(), (pt_, stage_gate[0])], inc=pv)
        pv_m = pv.now()
        for k in range(NU):
            transpose_evac(xnT[0:96, k * 128:(k + 1) * 128], wxT[:, k * 96:(k + 1) * 96],
                           eye_b[0:96, 0:96], extra_w=[pv_m, dcn_all])
        stage_gate[0] = pt_.n
        for j in range(ND):
            slot = j % 2
            st_f = xa[:, slot * 2 * D: slot * 2 * D + 64]
            st_b = xnT[:, slot * D: slot * D + 64]
            D_("sync", lambda e, st_f=st_f, j=j: dma(e, st_f, wdt[j * 128:(j + 1) * 128, :]),
               waits=[(pt_, stage_gate[0]), (dwt, dwt.n)], inc=dwt, incn=16)
            D_("vector", lambda e, st_b=st_b, st_f=st_f: e.tensor_copy(st_b, st_f),
               waits=[dwt.now(), (pt_, stage_gate[0])], inc=pv)
            pv_m = pv.now()
            bname = pT_next()
            pT = pTa if bname == "pTa" else pTb
            D_("tensor", lambda e, pT=pT, st_b=st_b: e.transpose(pT[0:64, 0:128], st_b, eye_b[:]),
               waits=[bk_acq(bname), pv_m, dcn_all], inc=pt_)
            mark[("w_dt", j)] = pt_.n
            D_("scalar", lambda e, pT=pT, j=j: e.copy(wdtT[0:64, j * 128:(j + 1) * 128], pT[0:64, 0:128]),
               waits=[pt_.now()], inc=ps_)
            bk_rel(bname, ps_)
            stage_gate[0] = pt_.n

        for eidx in range(NE):
            for k in range(NK):
                D_("tensor", lambda e, eidx=eidx, k=k:
                   e.matmul(pM[:, 0:1], winT[k][:, eidx * 128:(eidx + 1) * 128], bvb[:, k:k + 1],
                            start=(k == 0), stop=(k == NK - 1)),
                   waits=[bk_acq("pM"), (pv, pv.n), (ps_, ps_.n)] if k == 0 else (),
                   inc=pt_ if k == NK - 1 else None)
            D_("scalar", lambda e, eidx=eidx: e.copy(wb_s[:, eidx:eidx + 1], pM[:, 0:1]),
               waits=[pt_.now()], inc=ps_)
            bk_rel("pM", ps_)

        P.base = [dcn_all, (ps_, ps_.n), (pv, pv.n)]
        prep_t = (pt_, pt_.n)

        def x_dma(c):
            if c >= NC:
                return
            slot = c % 2
            dxs_ = dxp[c % 2]
            w = [(pt_, mark.get(("xT", c - 2), 0)), prep_t]
            for tau in range(2):
                D_("sync", lambda e, c=c, slot=slot, tau=tau:
                   dma(e, xa[:, slot * 2 * D + tau * D: slot * 2 * D + (tau + 1) * D],
                       xd[c * TP + tau * 128: c * TP + (tau + 1) * 128, :]),
                   waits=w, inc=dxs_, incn=16)
            mark[("xdma", c)] = (dxs_, dxs_.n)

        def chain(c):
            slot = c % 2

            def sl_ln():
                for tau in range(2):
                    xt = xa[:, slot * 2 * D + tau * D: slot * 2 * D + (tau + 1) * D]
                    col = slot * 2 + tau
                    D_("scalar", lambda e, xt=xt, col=col:
                       e.activation(sp4[:, 0:D], xt, AF.Square, accum_out=ln_q[:, col:col + 1]),
                       waits=[mark[("xdma", c)], ps_.now(),
                              (pv, mark.get(("dtadd", c - 1, ND - 1), 0))], inc=ps_)
                    D_("vector", lambda e, xt=xt, col=col:
                       e.tensor_reduce(ln_s[:, col:col + 1], xt, AX.X, ALU.add),
                       waits=[mark[("xdma", c)]], inc=pv)
                cs = slice(slot * 2, slot * 2 + 2)
                D_("vector", lambda e, cs=cs: e.tensor_scalar(ln_m[:, cs], ln_s[:, cs], 1.0 / D, None, ALU.mult),
                   waits=[ps_.now(), pv.now()], inc=pv)
                D_("vector", lambda e, cs=cs: e.tensor_scalar(ln_v[:, cs], ln_q[:, cs], 1.0 / D, None, ALU.mult),
                   waits=[pv.now()], inc=pv)
                D_("vector", lambda e, cs=cs: e.tensor_tensor(ln_t[:, cs], ln_m[:, cs], ln_m[:, cs], ALU.mult),
                   waits=[pv.now()], inc=pv)
                D_("vector", lambda e, cs=cs: e.tensor_tensor(ln_v[:, cs], ln_v[:, cs], ln_t[:, cs], ALU.subtract),
                   waits=[pv.now()], inc=pv)
                D_("vector", lambda e, cs=cs: e.tensor_scalar(ln_t[:, cs], ln_v[:, cs], 1.0 + EPS, EPS * EPS, ALU.mult, ALU.add),
                   waits=[pv.now()], inc=pv)
                D_("scalar", lambda e, cs=cs: e.activation(ln_v[:, cs], ln_t[:, cs], AF.Ln), waits=[pv.now()], inc=ps_)
                D_("scalar", lambda e, cs=cs: e.activation(q_s[:, cs], ln_v[:, cs], AF.Exp, scale=-0.5), inc=ps_)
                D_("vector", lambda e, cs=cs: e.tensor_tensor(nmq[:, cs], ln_m[:, cs], q_s[:, cs], ALU.mult),
                   waits=[ps_.now(), pv.now()], inc=pv)
                D_("vector", lambda e, cs=cs: e.tensor_scalar(ln_t[:, cs], nmq[:, cs], -1.0, None, ALU.mult),
                   waits=[pv.now()], inc=pv)
                for tau in range(2):
                    xt = xa[:, slot * 2 * D + tau * D: slot * 2 * D + (tau + 1) * D]
                    col = slot * 2 + tau
                    D_("vector", lambda e, xt=xt, col=col:
                       e.tensor_scalar(xt, xt, q_s[:, col:col + 1], ln_t[:, col:col + 1], ALU.mult, ALU.add),
                       waits=[pv.now()], inc=pv)
                mark[("norm", c)] = pv.n

            def sl_xt():
                for tau in range(2):
                    for k in range(NK):
                        src = xa[:, slot * 2 * D + tau * D + k * 128: slot * 2 * D + tau * D + (k + 1) * 128]
                        bname = pF_next()
                        pT = pA if bname == "pA" else pB
                        D_("tensor", lambda e, pT=pT, src=src: e.transpose(pT[:, 0:128], src, eye_f[:]),
                           waits=[bk_acq(bname), (pv, mark[("norm", c)])], inc=pt_)
                        D_("scalar", lambda e, pT=pT, tau=tau, k=k:
                           e.activation(xnT[:, k * TP + tau * 128: k * TP + tau * 128 + 128],
                                        pT[:, 0:128], AF.Identity, scale=gv_s[:, k:k + 1]),
                           waits=[pt_.now(), (pt_, mark.get(("win_mm", c - 1), 0))], inc=ps_)
                        bk_rel(bname, ps_)
                mark[("xT", c)] = pt_.n
                mark[("xnT", c)] = ps_.n

            def win_etile(eidx):
                bname = ("pA", "pB")[eidx % 2]
                pp = pA if bname == "pA" else pB
                for k in range(NK):
                    w = [(ps_, mark[("xnT", c)]), bk_acq(bname)] if k == 0 else ()
                    D_("tensor", lambda e, pp=pp, eidx=eidx, k=k:
                       e.matmul(pp[:, 0:TP], winT[k][:, eidx * 128:(eidx + 1) * 128],
                                xnT[:, k * TP:(k + 1) * TP], start=(k == 0), stop=(k == NK - 1)),
                       waits=w, inc=pt_ if k == NK - 1 else None)
                mm_mark = pt_.n
                if eidx == NE - 1:
                    mark[("win_mm", c)] = pt_.n
                if eidx < NU:
                    ue = uex[:, (eidx % 2) * (TP + 3): (eidx % 2 + 1) * (TP + 3)]
                    ua = uacc[:, (eidx % 2) * TP: (eidx % 2 + 1) * TP]
                    D_("vector", lambda e, ue=ue, eidx=eidx:
                       e.tensor_copy(ue[:, 0:3], halo[:, 3 * eidx:3 * eidx + 3]),
                       waits=[pv.now()], inc=pv)
                    hr = pv.now()
                    D_("scalar", lambda e, ue=ue, pp=pp, eidx=eidx:
                       e.activation(ue[:, 3:3 + TP], pp[:, 0:TP], AF.Identity, bias=wb_s[:, eidx:eidx + 1]),
                       waits=[(pt_, mm_mark), hr], inc=ps_)
                    bk_rel(bname, ps_)
                    D_("vector", lambda e, ue=ue, eidx=eidx:
                       e.tensor_scalar(ctmp[:], ue[:, 0:TP], cw_s[:, 4 * eidx:4 * eidx + 1], None, ALU.mult),
                       waits=[ps_.now()], inc=pv)
                    D_("vector", lambda e, ua=ua, ue=ue, eidx=eidx:
                       e.scalar_tensor_tensor(ua, ue[:, 1:1 + TP], cw_s[:, 4 * eidx + 1:4 * eidx + 2],
                                              ctmp[:], ALU.mult, ALU.add), inc=pv)
                    D_("vector", lambda e, ua=ua, ue=ue, eidx=eidx:
                       e.scalar_tensor_tensor(ctmp[:], ue[:, 2:2 + TP], cw_s[:, 4 * eidx + 2:4 * eidx + 3],
                                              ua, ALU.mult, ALU.add), inc=pv)
                    D_("vector", lambda e, ua=ua, ue=ue, eidx=eidx:
                       e.scalar_tensor_tensor(ua, ue[:, 3:3 + TP], cw_s[:, 4 * eidx + 3:4 * eidx + 4],
                                              ctmp[:], ALU.mult, ALU.add), inc=pv)
                    D_("vector", lambda e, ue=ue, eidx=eidx:
                       e.tensor_copy(halo[:, 3 * eidx:3 * eidx + 3], ue[:, TP:TP + 3]),
                       waits=[pv.now()], inc=pv)
                    mark[("conv", c, eidx)] = pv.n
                    if eidx < ND:
                        j = eidx
                        D_("scalar", lambda e, ua=ua, j=j:
                           e.activation(uh[:, slot * ND * TP + j * TP: slot * ND * TP + (j + 1) * TP],
                                        ua, AF.Silu, bias=cb_s[:, j:j + 1]),
                           waits=[(pv, mark[("conv", c, eidx)]),
                                  (pv, mark.get(("yD", c - 2, j), 0)),
                                  (pv, mark.get(("dt", c - 2, j), 0))], inc=ps_)
                        mark[("uh", c, j)] = ps_.n
                        mark[("ub", c, eidx)] = ps_.n
                    else:
                        D_("scalar", lambda e, ua=ua, eidx=eidx:
                           e.activation(ub_s[:, (eidx - ND) * TP:(eidx - ND + 1) * TP], ua, AF.Silu,
                                        bias=cb_s[:, eidx:eidx + 1]),
                           waits=[(pv, mark[("conv", c, eidx)]),
                                  (pt_, mark.get(("wx_mm", c - 1), 0))], inc=ps_)
                        mark[("ub", c, eidx)] = ps_.n
                else:
                    zi = eidx - NU
                    D_("scalar", lambda e, pp=pp, eidx=eidx, zi=zi:
                       e.activation(szs[:, slot * ND * TP + zi * TP: slot * ND * TP + (zi + 1) * TP],
                                    pp[:, 0:TP], AF.Silu, bias=wb_s[:, eidx:eidx + 1]),
                       waits=[(pt_, mm_mark), (pv, mark.get(("yg", c - 2, zi), 0))], inc=ps_)
                    bk_rel(bname, ps_)
                    if zi == ND - 1:
                        mark[("sz", c)] = ps_.n

            def sl_wx():
                for k in range(NU):
                    w = [(ps_, mark[("ub", c, k)])]
                    if k == 0:
                        w.append(bk_acq("pWx"))
                    rhs = (uh[:, slot * ND * TP + k * TP: slot * ND * TP + (k + 1) * TP]
                           if k < ND else ub_s[:, (k - ND) * TP:(k - ND + 1) * TP])
                    D_("tensor", lambda e, k=k, rhs=rhs:
                       e.matmul(pWx[0:96, 0:TP], wxT[:, k * 96:(k + 1) * 96],
                                rhs, start=(k == 0), stop=(k == NU - 1)),
                       waits=w, inc=pt_ if k == NU - 1 else None)
                mark[("wx_mm", c)] = pt_.n
                D_("scalar", lambda e: e.copy(dtr_b[0:64, :], pWx[0:64, 0:TP]),
                   waits=[(pt_, mark[("wx_mm", c)])], inc=ps_)
                mark[("dtr", c)] = ps_.n
                D_("scalar", lambda e: e.copy(bc_sb[64:96, :], pWx[64:96, 0:TP]),
                   waits=[(bcw, mark.get(("bcw", c - 1), 0))], inc=ps_)
                bk_rel("pWx", ps_)
                mark[("bc", c)] = ps_.n
                D_("sync", lambda e: dma(e, bscr[:, c * TP:(c + 1) * TP], bc_sb[64:96, :]),
                   waits=[(ps_, mark[("bc", c)])], inc=bcw, incn=16)
                mark[("bcw", c)] = bcw.n

            def wdt_tile(j):
                bname = ("pA", "pB")[j % 2]
                pp = pA if bname == "pA" else pB
                D_("tensor", lambda e, pp=pp, j=j:
                   e.matmul(pp[:, 0:TP], wdtT[0:64, j * 128:(j + 1) * 128], dtr_b[0:64, :],
                            start=True, stop=True),
                   waits=[(ps_, mark[("dtr", c)]), bk_acq(bname)], inc=pt_)
                mmk = pt_.n
                s0 = sp4[:, (j % 2) * 2 * TP:(j % 2) * 2 * TP + TP]
                s1 = sp4[:, (j % 2) * 2 * TP + TP:(j % 2) * 2 * TP + 2 * TP]
                D_("scalar", lambda e, pp=pp, j=j, s0=s0:
                   e.activation(s0, pp[:, 0:TP], AF.Abs, bias=bdt_s[:, j:j + 1]),
                   waits=[(pt_, mmk), (pv, mark.get(("dtadd", c, j - 2), 0))], inc=ps_)
                D_("scalar", lambda e, s0=s0, s1=s1: e.activation(s1, s0, AF.Exp, scale=-1.0), inc=ps_)
                D_("scalar", lambda e, s0=s0, s1=s1: e.activation(s0, s1, AF.Ln, bias=1.0), inc=ps_)
                D_("scalar", lambda e, pp=pp, j=j, s1=s1:
                   e.activation(s1, pp[:, 0:TP], AF.Relu, bias=bdt_s[:, j:j + 1]), inc=ps_)
                mark[("sp", c, j)] = ps_.n
                bk_rel(bname, ps_)
                dslot = c % 2
                dts = dt_s[:, dslot * ND * TP + j * TP: dslot * ND * TP + (j + 1) * TP]
                dtus = dtu_s[:, dslot * ND * TP + j * TP: dslot * ND * TP + (j + 1) * TP]
                D_("vector", lambda e, s0=s0, s1=s1, dts=dts: e.tensor_tensor(dts, s0, s1, ALU.add),
                   waits=[(ps_, mark[("sp", c, j)]), (ps_, mark.get(("aexp", c - 1), 0))], inc=pv)
                mark[("dtadd", c, j)] = pv.n
                D_("vector", lambda e, dts=dts, dtus=dtus, j=j:
                   e.tensor_tensor(dtus, dts, uh[:, dslot * ND * TP + j * TP: dslot * ND * TP + (j + 1) * TP],
                                   ALU.mult),
                   waits=[(ps_, mark[("uh", c, j)])], inc=pv)
                mark[("dt", c, j)] = pv.n

            def sl_reps():
                wrep = [(bcw, mark[("bcw", c)]), (pv, mark.get(("repfree", c - 1), 0))]
                D_("sync", lambda e:
                   dma(e, repB[:].rearrange("p (n t) -> p n t", n=NST),
                       bscr[0:NST, c * TP:(c + 1) * TP][None].broadcast_to((128, NST, TP))),
                   waits=wrep, inc=rep, incn=16)
                D_("sync", lambda e:
                   dma(e, repC[:].rearrange("p (n t) -> p n t", n=NST),
                       bscr[NST:2 * NST, c * TP:(c + 1) * TP][None].broadcast_to((128, NST, TP))),
                   waits=wrep, inc=rep, incn=16)
                mark[("rep", c)] = rep.n

            th = [sl_ln, sl_xt]
            for i in range(NE):
                th.append(lambda i=i: win_etile(i))
            th.append(sl_wx)
            for j in range(ND):
                th.append(lambda j=j: wdt_tile(j))
            th.append(sl_reps)
            return th

        def scan_dtile(c, j):
            dslot = c % 2
            dts = dt_s[:, dslot * ND * TP + j * TP: dslot * ND * TP + (j + 1) * TP]
            dtus = dtu_s[:, dslot * ND * TP + j * TP: dslot * ND * TP + (j + 1) * TP]
            D_("vector", lambda e, dtus=dtus:
               e.tensor_tensor(ball[:].rearrange("p (n t) -> p n t", n=NST),
                               dtus[:, None].broadcast_to((128, NST, TP)),
                               repB[:].rearrange("p (n t) -> p n t", n=NST), ALU.mult),
               waits=[(rep, mark[("rep", c)])], inc=pv)
            mark[("p1", c, j)] = pv.n
            for n in range(NST):
                g = (c * ND + j) * NST + n
                aslot = a_s[:, (n % 4) * TP:(n % 4 + 1) * TP]
                D_("scalar", lambda e, dts=dts, j=j, n=n, aslot=aslot:
                   e.activation(aslot, dts, AF.Exp, scale=asc[:, j * NST + n: j * NST + n + 1]),
                   waits=[(pv, mark[("dt", c, j)]), (pv, ttsm.get(g - 4, 0))], inc=ps_)
                am = ps_.n
                D_("vector", lambda e, n=n, j=j, aslot=aslot:
                   e.tensor_tensor_scan(hall[:, n * TP:(n + 1) * TP], aslot, ball[:, n * TP:(n + 1) * TP],
                                        states[:, n * ND + j: n * ND + j + 1], ALU.mult, ALU.add),
                   waits=[(ps_, am)], inc=pv)
                ttsm[g] = pv.n
            if j == ND - 1:
                mark[("aexp", c)] = ps_.n
            D_("vector", lambda e, j=j:
               e.tensor_copy(states[:, j: j + (NST - 1) * ND + 1: ND], hall[:, TP - 1:: TP]),
               waits=[pv.now()], inc=pv)
            D_("vector", lambda e:
               e.tensor_tensor(zall[:].rearrange("p (n t) -> p n t", n=NST),
                               hall[:].rearrange("p (n t) -> p n t", n=NST),
                               repC[:].rearrange("p (n t) -> p n t", n=NST), ALU.mult),
               inc=pv)
            if j == ND - 1:
                mark[("repfree", c)] = pv.n
            D_("vector", lambda e:
               e.tensor_tensor(ball[:, 0:8 * TP], zall[:, 0:8 * TP], zall[:, 8 * TP:16 * TP], ALU.add),
               inc=pv)
            D_("vector", lambda e:
               e.tensor_tensor(ball[:, 8 * TP:12 * TP], ball[:, 0:4 * TP], ball[:, 4 * TP:8 * TP], ALU.add),
               inc=pv)
            D_("vector", lambda e:
               e.tensor_tensor(ball[:, 12 * TP:14 * TP], ball[:, 8 * TP:10 * TP], ball[:, 10 * TP:12 * TP], ALU.add),
               inc=pv)
            D_("vector", lambda e:
               e.tensor_tensor(zred[:], ball[:, 12 * TP:13 * TP], ball[:, 13 * TP:14 * TP], ALU.add),
               waits=[pv.now()], inc=pv)
            D_("vector", lambda e, j=j:
               e.scalar_tensor_tensor(y_t[:], uh[:, dslot * ND * TP + j * TP: dslot * ND * TP + (j + 1) * TP],
                                      dsk_s[:, j:j + 1], zred[:], ALU.mult, ALU.add),
               waits=[(ps_, mark[("uh", c, j)])], inc=pv)
            mark[("yD", c, j)] = pv.n
            D_("vector", lambda e, j=j, dslot=dslot:
               e.tensor_tensor(yg_s[:, j * TP:(j + 1) * TP], y_t[:],
                               szs[:, dslot * ND * TP + j * TP: dslot * ND * TP + (j + 1) * TP], ALU.mult),
               waits=[(pt_, mark.get(("wout_mm", c - 1), 0)), (ps_, mark[("sz", c)])], inc=pv)
            mark[("yg", c, j)] = pv.n

        def emit_out(c):
            for dm in range(ND):
                bname = ("pA", "pB")[dm % 2]
                pp = pA if bname == "pA" else pB
                for k in range(ND):
                    w = [(pv, mark[("yg", c, ND - 1)]), bk_acq(bname)] if k == 0 else ()
                    D_("tensor", lambda e, pp=pp, dm=dm, k=k:
                       e.matmul(pp[:, 0:TP], woutT[k][:, dm * 128:(dm + 1) * 128],
                                yg_s[:, k * TP:(k + 1) * TP], start=(k == 0), stop=(k == ND - 1)),
                       waits=w, inc=pt_ if k == ND - 1 else None)
                mmk = pt_.n
                if dm == ND - 1:
                    mark[("wout_mm", c)] = pt_.n
                ofs = ofm[:, (dm % 2) * TP:(dm % 2 + 1) * TP]
                w_ofm = mark.get(("otT", c, dm - 2), mark.get(("otT", c - 1, ND + dm - 2), 0))
                D_("scalar", lambda e, pp=pp, ofs=ofs:
                   e.copy(ofs, pp[:, 0:TP]),
                   waits=[(pt_, mmk), (pt_, w_ofm)], inc=ps_)
                bk_rel(bname, ps_)
                ofm_m = ps_.n
                for tau in range(2):
                    bnT = pF_next()
                    pT = pA if bnT == "pA" else pB
                    D_("tensor", lambda e, pT=pT, ofs=ofs, tau=tau:
                       e.transpose(pT[:, 0:128], ofs[:, tau * 128:(tau + 1) * 128], eye_f[:]),
                       waits=[bk_acq(bnT), (ps_, ofm_m)], inc=pt_)
                    D_("scalar", lambda e, pT=pT, tau=tau, dm=dm:
                       e.copy(otm[:, tau * D + dm * 128: tau * D + (dm + 1) * 128], pT[:, 0:128]),
                       waits=[pt_.now(), (po, 32 * c)], inc=ps_)
                    bk_rel(bnT, ps_)
                    mark[("otm_e", c, dm, tau)] = ps_.n
                mark[("otT", c, dm)] = pt_.n
            for tau in range(2):
                D_("sync", lambda e, c=c, tau=tau:
                   dma(e, part[c * TP + tau * 128: c * TP + (tau + 1) * 128, :],
                       otm[:, tau * D:(tau + 1) * D]),
                   waits=[(ps_, mark[("otm_e", c, ND - 1, tau)])], inc=po, incn=16)

        x_dma(0)
        x_dma(1)
        for t in chain(0):
            t()
        for c in range(NC):
            x_dma(c + 2)
            th_next = chain(c + 1) if c + 1 < NC else None
            for j in range(ND):
                scan_dtile(c, j)
                if j == 1 and th_next is not None:
                    for t in th_next:
                        t()
            emit_out(c)

        if DBG[0]:
            dbg_specs = [("d_xnT", xnT, None), ("d_ub", ub_s, None), ("d_uh", uh, ND * TP),
                         ("d_dt", dt_s, ND * TP), ("d_dtu", dtu_s, ND * TP), ("d_szs", szs, ND * TP),
                         ("d_repB", repB, None), ("d_repC", repC, None),
                         ("d_hall", hall, None), ("d_yg", yg_s, None), ("d_states", states, None),
                         ("d_ball", ball, None), ("d_asc", asc, None), ("d_wb", wb_s, None),
                         ("d_xa", xa, 2 * D)]
            for nm, tile, w_ in dbg_specs:
                w_ = w_ or tile.shape[1]
                od = nc.dram_tensor(nm, [128, w_], tile.dtype, kind="ExternalOutput").ap()
                D_("sync", lambda e, od=od, tile=tile, w_=w_: dma(e, od[:, :], tile[0:128, 0:w_]),
                   waits=[(po, po.n)], inc=po, incn=16)
        D_("sync", lambda e: e.nop(), waits=[(po, po.n)])
        P.simulate()
        block = cm.enter_context(nc.Block())
        P.attach(block)
    return nc


DBG = [False]



def build_l2(TOK=2048, TC=128):
    NCH = TOK // TC
    NT = TC // 128
    nc = bass.Bass("TRN2", target_bir_lowering=False, debug=False, detect_race_conditions=False)
    dr = lambda n, s, dt, k: nc.dram_tensor(n, s, dt, kind=k).ap()
    xf = dr("xf", [TOK, D], F32, "ExternalInput")
    pr = [dr(f"p{i}", [TOK, D], F32, "ExternalInput") for i in range(4)]
    w1 = dr("w1", [HF, D], F32, "ExternalInput")
    b1 = dr("b1", [HF, 1], F32, "ExternalInput")
    w2 = dr("w2", [D, HF], F32, "ExternalInput")
    b2 = dr("b2", [D, 1], F32, "ExternalInput")
    eyeb = dr("eyeb", [128, 128], BF16, "ExternalInput")
    out = dr("o", [TOK, D], F32, "ExternalOutput")

    NH = HF // 128
    NK = D // 128

    cm = contextlib.ExitStack()
    with cm:
        bw1T = [_sb(nc, cm, f"bw1T{k}", [128, HF], BF16) for k in range(NK)]
        bw2T = [_sb(nc, cm, f"bw2T{k}", [128, D], BF16) for k in range(NH)]
        al1 = _sb(nc, cm, "al1", [128, NH], F32)
        al2p = _sb(nc, cm, "al2p", [128, 4 * NK], F32)
        al2 = _sb(nc, cm, "al2", [128, NK], F32)
        al2tA = _sb(nc, cm, "al2tA", [128, NK], F32)
        al2tB = _sb(nc, cm, "al2tB", [128, NK], F32)
        al2r = _sb(nc, cm, "al2r", [128, NK], F32)
        b1_s = _sb(nc, cm, "b1_s", [128, NH], F32)
        b2_s = _sb(nc, cm, "b2_s", [128, NK], F32)
        eye_b = _sb(nc, cm, "eye_b", [128, 128], BF16)
        wst = _sb(nc, cm, "wst", [128, 2 * D], F32)
        wsb = _sb(nc, cm, "wsb", [128, 2 * D], BF16)
        xs = _sb(nc, cm, "xs", [128, 2 * NT * D], F32)
        racc = _sb(nc, cm, "racc", [128, 2 * NT * D], F32)
        r1 = _sb(nc, cm, "r1", [128, 2 * NT * D], F32)
        sqg = _sb(nc, cm, "sqg", [128, D], F32)
        ln_s = _sb(nc, cm, "ln_s", [128, 2 * NT], F32)
        ln_q = _sb(nc, cm, "ln_q", [128, 2 * NT], F32)
        ln_m = _sb(nc, cm, "ln_m", [128, 2 * NT], F32)
        ln_v = _sb(nc, cm, "ln_v", [128, 2 * NT], F32)
        ln_t = _sb(nc, cm, "ln_t", [128, 2 * NT], F32)
        q1_s = _sb(nc, cm, "q1_s", [128, 2 * NT], F32)
        s1a = _sb(nc, cm, "s1a", [128, 2 * NT], F32)
        s1b = _sb(nc, cm, "s1b", [128, 2 * NT], F32)
        xn2 = _sb(nc, cm, "xn2", [128, NT * D], BF16)
        xnT2 = _sb(nc, cm, "xnT2", [128, NK * TC], BF16)
        g_all = _sb(nc, cm, "g_all", [128, NH * TC], BF16)
        ffm = _sb(nc, cm, "ffm", [128, 2 * TC], BF16)
        otm = _sb(nc, cm, "otm", [128, NT * D], F32)
        zzpad = _sb(nc, cm, "zzpad", [128, 96], F32)
        pA = _ps(nc, cm, "pA")
        pB = _ps(nc, cm, "pB")
        pTa = _ps(nc, cm, "pTa", BF16)
        pTb = _ps(nc, cm, "pTb", BF16)

        P = Plan()
        S = {k: Sem(nc, cm, k) for k in ("dx0", "dx1", "dwt", "dac", "po", "ps", "pv", "pt", "pg")}
        dxq = (S["dx0"], S["dx1"])
        dwt, dac, po = (S[k] for k in ("dwt", "dac", "po"))
        ps_, pv, pt_, pg = (S[k] for k in ("ps", "pv", "pt", "pg"))
        mark = {}
        banks = {b: (ps_, 0) for b in ("pA", "pB", "pTa", "pTb")}

        def bk_acq(b):
            s, v = banks[b]
            return (s, v)

        def bk_rel(b, sem):
            banks[b] = (sem, sem.n)

        tcyc = [0]

        def D_(eng, fn, waits=(), inc=None, incn=1):
            P.do(eng, fn, waits, inc, incn)

        def dma(e, dst, src):
            return e.dma_start(out=dst, in_=src)

        D_("sync", lambda e: dma(e, eye_b[:], eyeb[:]), inc=dwt, incn=16)
        for h in range(NH):
            D_("sync", lambda e, h=h: dma(e, b1_s[:, h:h + 1], b1[h * 128:(h + 1) * 128, :]), inc=dwt, incn=16)
        for k in range(NK):
            D_("sync", lambda e, k=k: dma(e, b2_s[:, k:k + 1], b2[k * 128:(k + 1) * 128, :]), inc=dwt, incn=16)
        dwt_c = dwt.now()

        def bin_block(src, r0, col0, alpha_dst, dst_sel, tag, ti):
            slot = ti % 2
            st_f = wst[:, slot * D: slot * D + D]
            st_b = wsb[:, slot * D: slot * D + D]
            D_("sync", lambda e, st_f=st_f: dma(e, st_f, src[r0:r0 + 128, col0:col0 + D]),
               waits=[(ps_, mark.get((tag, ti - 2), 0)), (pv, mark.get((tag, "v", ti - 2), 0)),
                      (dwt, dwt.n)], inc=dwt, incn=16)
            D_("vector", lambda e, st_f=st_f, alpha_dst=alpha_dst:
               e.tensor_reduce(alpha_dst, st_f, AX.X, ALU.add, apply_absolute_value=True),
               waits=[dwt.now()], inc=pv)
            mark[(tag, "v", ti)] = pv.n
            D_("scalar", lambda e, st_f=st_f, st_b=st_b: e.activation(st_b, st_f, AF.Sign),
               waits=[dwt.now(), pv.now()], inc=ps_)
            sg = ps_.now()
            for k in range(NK):
                bname = ("pTa", "pTb")[tcyc[0] % 2]
                tcyc[0] += 1
                pT = pTa if bname == "pTa" else pTb
                dst, dcol = dst_sel(k)
                D_("tensor", lambda e, pT=pT, st_b=st_b, k=k:
                   e.transpose(pT[:, 0:128], st_b[:, k * 128:(k + 1) * 128], eye_b[:]),
                   waits=[bk_acq(bname), sg, dwt_c], inc=pt_)
                D_("scalar", lambda e, pT=pT, dst=dst, dcol=dcol:
                   e.copy(dst[:, dcol:dcol + 128], pT[:, 0:128]),
                   waits=[pt_.now()], inc=ps_)
                bk_rel(bname, ps_)
            mark[(tag, ti)] = ps_.n

        ti = 0
        for r in range(NH):
            bin_block(w1, r * 128, 0, al1[:, r:r + 1], lambda k, r=r: (bw1T[k], r * 128), "bb", ti)
            ti += 1
        for r in range(NK):
            for bcol in range(4):
                bin_block(w2, r * 128, bcol * D, al2p[:, bcol * NK + r:bcol * NK + r + 1],
                          lambda k, r=r, bcol=bcol: (bw2T[bcol * 8 + k], r * 128), "bb", ti)
                ti += 1
        D_("vector", lambda e: e.tensor_copy(al2tA[:, 0:NK], al1[:, 0:NK]), inc=pv)
        D_("vector", lambda e: e.tensor_scalar(al1[:], al1[:], 1.0 / D, None, ALU.mult), waits=[pv.now()], inc=pv)
        D_("vector", lambda e: e.tensor_tensor(al2tA[:], al2p[:, 0:NK], al2p[:, NK:2 * NK], ALU.add), waits=[pv.now()], inc=pv)
        D_("vector", lambda e: e.tensor_tensor(al2tB[:], al2p[:, 2 * NK:3 * NK], al2p[:, 3 * NK:4 * NK], ALU.add), waits=[pv.now()], inc=pv)
        D_("vector", lambda e: e.tensor_tensor(al2r[:], al2tA[:], al2tB[:], ALU.add), waits=[pv.now()], inc=pv)
        D_("vector", lambda e: e.tensor_scalar(al2[:], al2r[:], 1.0 / HF, None, ALU.mult), waits=[pv.now()], inc=pv)
        P.base = [dwt_c, (pv, pv.n)]

        def xp_dma(c):
            if c >= NCH:
                return
            slot = c % 2
            w = [(pv, mark.get(("r1done", c - 2), 0)), (po, max(0, (c - 1)) * NT * 16)]
            dxs_ = dxq[c % 2]
            D_("sync", lambda e, c=c, slot=slot:
               dma(e, xs[:, slot * NT * D:(slot + 1) * NT * D].rearrange("p (t d) -> p t d", d=D),
                   xf[c * TC:(c + 1) * TC, :].rearrange("(t p) d -> p t d", p=128)),
               waits=w, inc=dxs_, incn=16)
            mark[("xdma", c)] = (dxs_, dxs_.n)
            rs = racc[:, slot * NT * D:(slot + 1) * NT * D].rearrange("p (t d) -> p t d", d=D)
            for i in range(4):
                D_("gpsimd", lambda e, i=i, c=c, rs=rs:
                   e.dma_start(out=rs, in_=pr[i][c * TC:(c + 1) * TC, :].rearrange("(t p) d -> p t d", p=128),
                               accum_op=(ALU.bypass if i == 0 else ALU.add)),
                   waits=w + [(dac, dac.n)], inc=dac, incn=16)
            mark[("pacc", c)] = dac.n

        xp_dma(0)
        xp_dma(1)
        for c in range(NCH):
            slot = c % 2
            xc = xs[:, slot * NT * D:(slot + 1) * NT * D]
            rc = racc[:, slot * NT * D:(slot + 1) * NT * D]
            r1c = r1[:, slot * NT * D:(slot + 1) * NT * D]
            cs = slice(slot * NT, slot * NT + NT)
            for tau in range(NT):
                xt = xc[:, tau * D:(tau + 1) * D]
                col = slot * NT + tau
                D_("scalar", lambda e, xt=xt, col=col:
                   e.activation(sqg[:], xt, AF.Square, accum_out=ln_q[:, col:col + 1]),
                   waits=[mark[("xdma", c)], ps_.now(), pv.now()], inc=ps_)
                D_("vector", lambda e, xt=xt, col=col:
                   e.tensor_reduce(ln_s[:, col:col + 1], xt, AX.X, ALU.add),
                   waits=[mark[("xdma", c)]], inc=pv)
            D_("vector", lambda e, cs=cs: e.tensor_scalar(ln_m[:, cs], ln_s[:, cs], 1.0 / D, None, ALU.mult),
               waits=[ps_.now(), pv.now()], inc=pv)
            D_("vector", lambda e, cs=cs: e.tensor_scalar(ln_v[:, cs], ln_q[:, cs], 1.0 / D, None, ALU.mult), waits=[pv.now()], inc=pv)
            D_("vector", lambda e, cs=cs: e.tensor_tensor(ln_t[:, cs], ln_m[:, cs], ln_m[:, cs], ALU.mult), waits=[pv.now()], inc=pv)
            D_("vector", lambda e, cs=cs: e.tensor_tensor(ln_v[:, cs], ln_v[:, cs], ln_t[:, cs], ALU.subtract), waits=[pv.now()], inc=pv)
            D_("vector", lambda e, cs=cs: e.tensor_scalar(ln_t[:, cs], ln_v[:, cs], 1.0, EPS, ALU.mult, ALU.add), waits=[pv.now()], inc=pv)
            D_("scalar", lambda e, cs=cs: e.activation(ln_v[:, cs], ln_t[:, cs], AF.Ln), waits=[pv.now()], inc=ps_)
            D_("scalar", lambda e, cs=cs: e.activation(q1_s[:, cs], ln_v[:, cs], AF.Exp, scale=-0.5), inc=ps_)
            D_("vector", lambda e, cs=cs: e.tensor_scalar(s1a[:, cs], q1_s[:, cs], 1.0, None, ALU.add),
               waits=[ps_.now(), pv.now()], inc=pv)
            D_("vector", lambda e, cs=cs: e.tensor_tensor(s1b[:, cs], ln_m[:, cs], q1_s[:, cs], ALU.mult), waits=[pv.now()], inc=pv)
            for tau in range(NT):
                col = slot * NT + tau
                D_("vector", lambda e, tau=tau, col=col, xc=xc:
                   e.tensor_scalar(xc[:, tau * D:(tau + 1) * D], xc[:, tau * D:(tau + 1) * D],
                                   s1a[:, col:col + 1], s1b[:, col:col + 1], ALU.mult, ALU.subtract),
                   inc=pv)
            D_("vector", lambda e, r1c=r1c, xc=xc, rc=rc: e.tensor_tensor(r1c, xc, rc, ALU.add),
               waits=[(dac, mark[("pacc", c)])], inc=pv)
            mark[("r1", c)] = pv.n
            for tau in range(NT):
                col = slot * NT + tau
                D_("scalar", lambda e, tau=tau, col=col, r1c=r1c:
                   e.activation(sqg[:], r1c[:, tau * D:(tau + 1) * D], AF.Square, accum_out=ln_q[:, col:col + 1]),
                   waits=[(pv, mark[("r1", c)]), ps_.now(), pv.now()], inc=ps_)
                D_("vector", lambda e, tau=tau, col=col, r1c=r1c:
                   e.tensor_reduce(ln_s[:, col:col + 1], r1c[:, tau * D:(tau + 1) * D], AX.X, ALU.add),
                   inc=pv)
            D_("vector", lambda e, cs=cs: e.tensor_scalar(ln_m[:, cs], ln_s[:, cs], 1.0 / D, None, ALU.mult),
               waits=[ps_.now(), pv.now()], inc=pv)
            D_("vector", lambda e, cs=cs: e.tensor_scalar(ln_v[:, cs], ln_q[:, cs], 1.0 / D, None, ALU.mult), waits=[pv.now()], inc=pv)
            D_("vector", lambda e, cs=cs: e.tensor_tensor(ln_t[:, cs], ln_m[:, cs], ln_m[:, cs], ALU.mult), waits=[pv.now()], inc=pv)
            D_("vector", lambda e, cs=cs: e.tensor_tensor(ln_v[:, cs], ln_v[:, cs], ln_t[:, cs], ALU.subtract), waits=[pv.now()], inc=pv)
            D_("vector", lambda e, cs=cs: e.tensor_scalar(ln_t[:, cs], ln_v[:, cs], 1.0, EPS, ALU.mult, ALU.add), waits=[pv.now()], inc=pv)
            D_("scalar", lambda e, cs=cs: e.activation(ln_v[:, cs], ln_t[:, cs], AF.Ln), waits=[pv.now()], inc=ps_)
            D_("scalar", lambda e, cs=cs: e.activation(q1_s[:, cs], ln_v[:, cs], AF.Exp, scale=-0.5), inc=ps_)
            for tau in range(NT):
                col = slot * NT + tau
                D_("vector", lambda e, tau=tau, col=col, r1c=r1c:
                   e.tensor_scalar(xn2[:, tau * D:(tau + 1) * D], r1c[:, tau * D:(tau + 1) * D],
                                   ln_m[:, col:col + 1], q1_s[:, col:col + 1], ALU.subtract, ALU.mult),
                   waits=[ps_.now(), (ps_, mark.get(("xnTe", c - 1), 0))], inc=pv)
            mark[("xn2", c)] = pv.n
            for tau in range(NT):
                for k in range(NK):
                    bname = ("pTa", "pTb")[tcyc[0] % 2]
                    tcyc[0] += 1
                    pT = pTa if bname == "pTa" else pTb
                    D_("tensor", lambda e, pT=pT, tau=tau, k=k:
                       e.transpose(pT[:, 0:128], xn2[:, tau * D + k * 128: tau * D + (k + 1) * 128], eye_b[:]),
                       waits=[bk_acq(bname), (pv, mark[("xn2", c)])], inc=pt_)
                    D_("scalar", lambda e, pT=pT, tau=tau, k=k:
                       e.copy(xnT2[:, k * TC + tau * 128: k * TC + tau * 128 + 128], pT[:, 0:128]),
                       waits=[pt_.now()], inc=ps_)
                    bk_rel(bname, ps_)
            mark[("xnTe", c)] = ps_.n
            for h in range(NH):
                bname = ("pA", "pB")[h % 2]
                pp = pA if bname == "pA" else pB
                for k in range(NK):
                    w = [(ps_, mark[("xnTe", c)]), bk_acq(bname)] if k == 0 else ()
                    D_("tensor", lambda e, pp=pp, h=h, k=k:
                       e.matmul(pp[:, 0:TC], bw1T[k][:, h * 128:(h + 1) * 128],
                                xnT2[:, k * TC:(k + 1) * TC], start=(k == 0), stop=(k == NK - 1)),
                       waits=w, inc=pt_ if k == NK - 1 else None)
                mmk = pt_.n
                if h == NH - 1:
                    mark[("w1_mm", c)] = pt_.n
                D_("scalar", lambda e, pp=pp, h=h:
                   e.activation(g_all[:, h * TC:(h + 1) * TC], pp[:, 0:TC], AF.Gelu,
                                bias=b1_s[:, h:h + 1], scale=al1[:, h:h + 1]),
                   waits=[(pt_, mmk), (pt_, mark.get(("w2_mm", c - 1), 0))], inc=ps_)
                bk_rel(bname, ps_)
            mark[("gall", c)] = ps_.n
            for dm in range(NK):
                bname = ("pA", "pB")[dm % 2]
                pp = pA if bname == "pA" else pB
                for k in range(NH):
                    w = [(ps_, mark[("gall", c)]), bk_acq(bname)] if k == 0 else ()
                    D_("tensor", lambda e, pp=pp, dm=dm, k=k:
                       e.matmul(pp[:, 0:TC], bw2T[k][:, dm * 128:(dm + 1) * 128],
                                g_all[:, k * TC:(k + 1) * TC], start=(k == 0), stop=(k == NH - 1)),
                       waits=w, inc=pt_ if k == NH - 1 else None)
                mmk = pt_.n
                if dm == NK - 1:
                    mark[("w2_mm", c)] = pt_.n
                ffs = ffm[:, (dm % 2) * TC:(dm % 2 + 1) * TC]
                w_ffm = mark.get(("oT", c, dm - 2), mark.get(("oT", c - 1, NK + dm - 2), 0))
                D_("scalar", lambda e, pp=pp, ffs=ffs, dm=dm:
                   e.activation(ffs, pp[:, 0:TC], AF.Identity, bias=b2_s[:, dm:dm + 1], scale=al2[:, dm:dm + 1]),
                   waits=[(pt_, mmk), (pt_, w_ffm)], inc=ps_)
                bk_rel(bname, ps_)
                ffm_m = ps_.n
                for tau in range(NT):
                    bnT = ("pTa", "pTb")[tcyc[0] % 2]
                    tcyc[0] += 1
                    pT = pTa if bnT == "pTa" else pTb
                    D_("tensor", lambda e, pT=pT, ffs=ffs, tau=tau:
                       e.transpose(pT[:, 0:128], ffs[:, tau * 128:(tau + 1) * 128], eye_b[:]),
                       waits=[bk_acq(bnT), (ps_, ffm_m)], inc=pt_)
                    D_("scalar", lambda e, pT=pT, tau=tau, dm=dm:
                       e.copy(otm[:, tau * D + dm * 128: tau * D + (dm + 1) * 128], pT[:, 0:128]),
                       waits=[pt_.now(), (po, NT * 16 * c)], inc=ps_)
                    bk_rel(bnT, ps_)
                    mark[("ote", c, dm, tau)] = ps_.n
                mark[("oT", c, dm)] = pt_.n
            for tau in range(NT):
                D_("vector", lambda e, tau=tau, r1c=r1c, rc=rc:
                   e.tensor_tensor(rc[:, tau * D:(tau + 1) * D], otm[:, tau * D:(tau + 1) * D],
                                   r1c[:, tau * D:(tau + 1) * D], ALU.add),
                   waits=[(ps_, mark[("ote", c, NK - 1, tau)])], inc=pv)
                mark[("oadd", c, tau)] = pv.n
            mark[("r1done", c)] = pv.n
            for tau in range(NT):
                D_("sync", lambda e, c=c, tau=tau, rc=rc:
                   dma(e, out[c * TC + tau * 128: c * TC + (tau + 1) * 128, :], rc[:, tau * D:(tau + 1) * D]),
                   waits=[(pv, mark[("oadd", c, tau)])], inc=po, incn=16)
            xp_dma(c + 2)

        if DBG[0]:
            dbg2 = [("e_al1", al1), ("e_al2", al2), ("e_al2p", al2p), ("e_al2r", al2r), ("e_al2tA", al2tA), ("e_al2tB", al2tB), ("e_bw1T0", bw1T[0]), ("e_bw2T0", bw2T[0]),
                    ("e_racc", racc), ("e_r1", r1), ("e_xn2", xn2), ("e_xnT2", xnT2),
                    ("e_gall", g_all), ("e_otm", otm), ("e_q1", q1_s), ("e_lnm", ln_m)]
            for nm, tile in dbg2:
                od = nc.dram_tensor(nm, [128, tile.shape[1]], tile.dtype, kind="ExternalOutput").ap()
                D_("sync", lambda e, od=od, tile=tile: dma(e, od[:, :], tile[0:128, :]),
                   waits=[(po, po.n)], inc=po, incn=16)
        D_("sync", lambda e: e.nop(), waits=[(po, po.n)])
        P.simulate()
        block = cm.enter_context(nc.Block())
        P.attach(block)
    return nc



def _l1_in_maps(inp, x, eyef, eyeb):
    f = lambda k: np.ascontiguousarray(np.asarray(inp[k], dtype=np.float32))
    in_maps = []
    for c in range(8):
        b, dire, dh = c >> 2, (c >> 1) & 1, c & 1
        pfx = "f" if dire == 0 else "b"
        xdir = x[b] if dire == 0 else x[b, ::-1]
        winf = f(f"L0_attn_{pfx}_Win")
        u_rows = winf[0:E]
        z_rows = winf[E + dh * EH: E + (dh + 1) * EH]
        cwm = f(f"L0_attn_{pfx}_convw").reshape(E, 4)
        cbm = f(f"L0_attn_{pfx}_convb").reshape(E, 1)
        wxm = f(f"L0_attn_{pfx}_Wx")
        if dh == 1:
            u_rows = np.concatenate([u_rows[EH:], u_rows[0:EH]], axis=0)
            cwm = np.concatenate([cwm[EH:], cwm[0:EH]], axis=0)
            cbm = np.concatenate([cbm[EH:], cbm[0:EH]], axis=0)
            wxm = np.concatenate([wxm[:, EH:], wxm[:, 0:EH]], axis=1)
        m = {
            "xd": np.ascontiguousarray(xdir),
            "win": np.ascontiguousarray(np.concatenate([u_rows, z_rows], axis=0)),
            "convw": np.ascontiguousarray(cwm),
            "convb": np.ascontiguousarray(cbm),
            "wx": np.ascontiguousarray(wxm),
            "wdt": np.ascontiguousarray(f(f"L0_attn_{pfx}_Wdt")[dh * EH:(dh + 1) * EH]),
            "bdt": np.ascontiguousarray(f(f"L0_attn_{pfx}_bdt").reshape(E, 1)[dh * EH:(dh + 1) * EH]),
            "alog": np.ascontiguousarray(f(f"L0_attn_{pfx}_Alog")[dh * EH:(dh + 1) * EH]),
            "dsk": np.ascontiguousarray(f(f"L0_attn_{pfx}_D").reshape(E, 1)[dh * EH:(dh + 1) * EH]),
            "wout": np.ascontiguousarray(f(f"L0_attn_{pfx}_Wout")[:, dh * EH:(dh + 1) * EH]),
            "gvec": f("L0_attn_norm_g").reshape(D, 1),
            "bvec": f("L0_attn_norm_b").reshape(D, 1),
            "eyef": eyef, "eyeb": eyeb,
        }
        in_maps.append(m)
    return in_maps


def kernel(**inp):
    f = lambda k: np.ascontiguousarray(np.asarray(inp[k], dtype=np.float32))
    x = f("L0_attn_x")
    L = x.shape[1]
    eyef = np.eye(128, dtype=np.float32)
    eyeb = np.eye(128).astype(BF16_NP)

    nc1 = build_l1(L)
    in_maps = _l1_in_maps(inp, x, eyef, eyeb)
    res1 = run_bass_kernel_spmd(nc1, in_maps, core_ids=list(range(8)))
    parts = []
    for c in range(8):
        p = np.asarray(res1.results[c]["part"], dtype=np.float32)
        if (c >> 1) & 1:
            p = p[::-1]
        parts.append(np.ascontiguousarray(p))

    nc2 = build_l2()
    in_maps2 = []
    for k in range(8):
        bk = (k * 2048) // L
        lo = k * 2048 - bk * L
        sl = slice(lo, lo + 2048)
        m = {
            "xf": np.ascontiguousarray(x[bk][sl]),
            "w1": f("L0_ffn_w1"),
            "b1": f("L0_ffn_b1").reshape(HF, 1),
            "w2": f("L0_ffn_w2"),
            "b2": f("L0_ffn_b2").reshape(D, 1),
            "eyeb": eyeb,
        }
        for i in range(4):
            m[f"p{i}"] = parts[bk * 4 + i][sl]
        in_maps2.append(m)
    res2 = run_bass_kernel_spmd(nc2, in_maps2, core_ids=list(range(8)))
    outs = [np.asarray(res2.results[k]["o"], dtype=np.float32) for k in range(8)]
    return np.concatenate(outs, axis=0).reshape(2, L, D)


# version 07 (final)
# speedup vs baseline: 1.8248x; 1.0196x over previous
import contextlib
import numpy as np
import ml_dtypes
import concourse.bass as bass
import concourse.mybir as mybir
from concourse.bass_utils import run_bass_kernel_spmd

F32 = mybir.dt.float32
BF16 = mybir.dt.bfloat16
AF = mybir.ActivationFunctionType
ALU = mybir.AluOpType
AX = mybir.AxisListType

D = 1024
E = 2048
EH = 1024
NST = 16
RNK = 64
HF = 4096
TP = 256
EPS = 1e-5
ENGINES = ("sync", "scalar", "vector", "tensor", "gpsimd")

BF16_NP = ml_dtypes.bfloat16


class Sem:
    def __init__(self, nc, cm, name):
        self.h = cm.enter_context(nc.semaphore(name))
        self.name = name
        self.n = 0

    def now(self):
        return (self, self.n)


class Plan:
    def __init__(self):
        self.ops = {e: [] for e in ENGINES}
        self.base = []

    def do(self, eng, fn, waits=(), inc=None, incn=1):
        waits = tuple((s, v) for (s, v) in tuple(waits) + tuple(self.base) if v > 0)
        if inc is not None:
            inc.n += incn
        self.ops[eng].append((waits, fn, (inc, incn) if inc is not None else None))

    def simulate(self):
        counts = {}
        ptr = {e: 0 for e in ENGINES}
        total = sum(len(v) for v in self.ops.values())
        done = 0
        progress = True
        while progress:
            progress = False
            for e in ENGINES:
                while ptr[e] < len(self.ops[e]):
                    waits, _, inc = self.ops[e][ptr[e]]
                    if any(counts.get(s.name, 0) < v for s, v in waits):
                        break
                    if inc is not None:
                        s, n = inc
                        counts[s.name] = counts.get(s.name, 0) + n
                    ptr[e] += 1
                    done += 1
                    progress = True
        if done != total:
            stuck = {e: (ptr[e], len(self.ops[e])) for e in ENGINES}
            detail = {}
            for e in ENGINES:
                if ptr[e] < len(self.ops[e]):
                    waits, _, _ = self.ops[e][ptr[e]]
                    detail[e] = [(s.name, v, counts.get(s.name, 0)) for s, v in waits]
            raise RuntimeError(f"plan deadlock: {stuck} waiting {detail}")

    def attach(self, block):
        def mk(name):
            def body(e):
                last = {}
                for waits, fn, inc in self.ops[name]:
                    for s, v in waits:
                        if v > last.get(s.name, -1):
                            e.wait_ge(s.h, v)
                            last[s.name] = v
                    ins = fn(e)
                    if inc is not None:
                        s, n = inc
                        ins.then_inc(s.h, n)
            return body
        block.sync(mk("sync"))
        block.scalar(mk("scalar"))
        block.vector(mk("vector"))
        block.tensor(mk("tensor"))
        block.gpsimd(mk("gpsimd"))


def _sb(nc, cm, name, shape, dt):
    return cm.enter_context(nc.sbuf_tensor(name, shape, dt))


def _ps(nc, cm, name, dt=F32):
    n = 512 if dt == F32 else 1024
    return cm.enter_context(nc.psum_tensor(name, [128, n], dt))



def build_l1(L=8192):
    NC = L // TP
    nc = bass.Bass("TRN2", target_bir_lowering=False, debug=False, detect_race_conditions=False)
    dr = lambda n, s, dt, k: nc.dram_tensor(n, s, dt, kind=k).ap()
    xd = dr("xd", [L, D], F32, "ExternalInput")
    win = dr("win", [E + EH, D], F32, "ExternalInput")
    convw = dr("convw", [E, 4], F32, "ExternalInput")
    convb = dr("convb", [E, 1], F32, "ExternalInput")
    wx = dr("wx", [RNK + 2 * NST, E], F32, "ExternalInput")
    wdt = dr("wdt", [EH, RNK], F32, "ExternalInput")
    bdt = dr("bdt", [EH, 1], F32, "ExternalInput")
    alog = dr("alog", [EH, NST], F32, "ExternalInput")
    dsk = dr("dsk", [EH, 1], F32, "ExternalInput")
    wout = dr("wout", [D, EH], F32, "ExternalInput")
    gvec = dr("gvec", [D, 1], F32, "ExternalInput")
    bvec = dr("bvec", [D, 1], F32, "ExternalInput")
    eyef = dr("eyef", [128, 128], F32, "ExternalInput")
    eyeb = dr("eyeb", [128, 128], BF16, "ExternalInput")
    part = dr("part", [L, D], F32, "ExternalOutput")
    bscr = dr("bscr", [2 * NST, L], BF16, "Internal")

    NU = E // 128
    NE = (E + EH) // 128
    ND = EH // 128
    NK = D // 128

    cm = contextlib.ExitStack()
    with cm:
        winT = [_sb(nc, cm, f"winT{k}", [128, NE * 128], BF16) for k in range(NK)]
        woutT = [_sb(nc, cm, f"woutT{k}", [128, D], BF16) for k in range(ND)]
        wxT = _sb(nc, cm, "wxT", [128, NU * 96], BF16)
        wdtT = _sb(nc, cm, "wdtT", [64, EH], BF16)
        asc = _sb(nc, cm, "asc", [128, ND * NST], F32)
        gv_s = _sb(nc, cm, "gv_s", [128, NK], F32)
        bvb = _sb(nc, cm, "bvb", [128, NK], BF16)
        bv_s = _sb(nc, cm, "bv_s", [128, NK], F32)
        bdt_s = _sb(nc, cm, "bdt_s", [128, ND], F32)
        dsk_s = _sb(nc, cm, "dsk_s", [128, ND], F32)
        cb_s = _sb(nc, cm, "cb_s", [128, NU], F32)
        cw_s = _sb(nc, cm, "cw_s", [128, NU * 4], F32)
        wb_s = _sb(nc, cm, "wb_s", [128, NE], F32)
        eye_f = _sb(nc, cm, "eye_f", [128, 128], F32)
        eye_b = _sb(nc, cm, "eye_b", [128, 128], BF16)
        xa = _sb(nc, cm, "xa", [128, 2 * 2 * D], F32)
        ln_s = _sb(nc, cm, "ln_s", [128, 4], F32)
        ln_q = _sb(nc, cm, "ln_q", [128, 4], F32)
        ln_m = _sb(nc, cm, "ln_m", [128, 4], F32)
        ln_t = _sb(nc, cm, "ln_t", [128, 4], F32)
        ln_v = _sb(nc, cm, "ln_v", [128, 4], F32)
        q_s = _sb(nc, cm, "q_s", [128, 4], F32)
        nmq = _sb(nc, cm, "nmq", [128, 4], F32)
        xnT = _sb(nc, cm, "xnT", [128, NK * TP], BF16)
        uex = _sb(nc, cm, "uex", [128, 2 * (TP + 3)], F32)
        uacc = _sb(nc, cm, "uacc", [128, 2 * TP], F32)
        ctmp = _sb(nc, cm, "ctmp", [128, TP], F32)
        halo = _sb(nc, cm, "halo", [128, NU * 3], F32)
        ub_s = _sb(nc, cm, "ub_s", [128, (NU - ND) * TP], BF16)
        uh = _sb(nc, cm, "uh", [128, 2 * ND * TP], BF16)
        dt_s = _sb(nc, cm, "dt_s", [128, 2 * ND * TP], F32)
        dtu_s = _sb(nc, cm, "dtu_s", [128, 2 * ND * TP], BF16)
        szs = _sb(nc, cm, "szs", [128, 2 * ND * TP], BF16)
        dtr_b = _sb(nc, cm, "dtr_b", [64, TP], BF16)
        bc_sb = _sb(nc, cm, "bc_sb", [128, TP], BF16)
        repB = _sb(nc, cm, "repB", [128, NST * TP], BF16)
        repC = _sb(nc, cm, "repC", [128, NST * TP], BF16)
        ball = _sb(nc, cm, "ball", [128, NST * TP], BF16)
        hall = _sb(nc, cm, "hall", [128, NST * TP], BF16)
        zall = _sb(nc, cm, "zall", [128, NST * TP], BF16)
        a_s = _sb(nc, cm, "a_s", [128, 3 * TP], F32)
        sp4 = _sb(nc, cm, "sp4", [128, 4 * TP], F32)
        y_t = _sb(nc, cm, "y_t", [128, TP], F32)
        yg_s = _sb(nc, cm, "yg_s", [128, 2 * ND * TP], BF16)
        ofm = _sb(nc, cm, "ofm", [128, 2 * TP], F32)
        otm = _sb(nc, cm, "otm", [128, 2 * D], F32)
        states = _sb(nc, cm, "states", [128, 128], F32)
        zzpad = _sb(nc, cm, "zzpad", [128, 24], F32)
        pA = _ps(nc, cm, "pA")
        pB = _ps(nc, cm, "pB")
        pTa = _ps(nc, cm, "pTa", BF16)
        pTb = _ps(nc, cm, "pTb", BF16)
        pWx = _ps(nc, cm, "pWx")
        pM = _ps(nc, cm, "pM")

        P = Plan()
        S = {k: Sem(nc, cm, k) for k in ("dx0", "dx1", "dwt", "dcn", "bcw", "rep", "po",
                                         "ps", "pv", "pt", "pg")}
        dxp = (S["dx0"], S["dx1"])
        dwt, dcn, bcw, rep, po = (S[k] for k in ("dwt", "dcn", "bcw", "rep", "po"))
        ps_, pv, pt_, pg = (S[k] for k in ("ps", "pv", "pt", "pg"))
        mark = {}
        ttsm = {}
        banks = {b: (ps_, 0) for b in ("pA", "pB", "pTa", "pTb", "pWx", "pM")}

        def bk_acq(b):
            s, v = banks[b]
            return (s, v)

        def bk_rel(b, sem):
            banks[b] = (sem, sem.n)

        tcyc = [0]
        tcycf = [0]

        def pT_next():
            t = ("pTa", "pTb")[tcyc[0] % 2]
            tcyc[0] += 1
            return t

        def pF_next():
            t = ("pA", "pB")[tcycf[0] % 2]
            tcycf[0] += 1
            return t

        def D_(eng, fn, waits=(), inc=None, incn=1):
            P.do(eng, fn, waits, inc, incn)

        def dma(e, dst, src):
            return e.dma_start(out=dst, in_=src)

        D_("sync", lambda e: dma(e, eye_f[:], eyef[:]), inc=dcn, incn=16)
        D_("sync", lambda e: dma(e, eye_b[:], eyeb[:]), inc=dcn, incn=16)
        for k in range(NK):
            D_("sync", lambda e, k=k: dma(e, gv_s[:, k:k + 1], gvec[k * 128:(k + 1) * 128, :]), inc=dcn, incn=16)
            D_("sync", lambda e, k=k: dma(e, bv_s[:, k:k + 1], bvec[k * 128:(k + 1) * 128, :]), inc=dcn, incn=16)
        for j in range(ND):
            D_("sync", lambda e, j=j: dma(e, bdt_s[:, j:j + 1], bdt[j * 128:(j + 1) * 128, :]), inc=dcn, incn=16)
            D_("sync", lambda e, j=j: dma(e, dsk_s[:, j:j + 1], dsk[j * 128:(j + 1) * 128, :]), inc=dcn, incn=16)
            D_("sync", lambda e, j=j: dma(e, asc[:, j * NST:(j + 1) * NST], alog[j * 128:(j + 1) * 128, :]), inc=dcn, incn=16)
        for t in range(NU):
            D_("sync", lambda e, t=t: dma(e, cb_s[:, t:t + 1], convb[t * 128:(t + 1) * 128, :]), inc=dcn, incn=16)
            D_("sync", lambda e, t=t: dma(e, cw_s[:, 4 * t:4 * t + 4], convw[t * 128:(t + 1) * 128, :]), inc=dcn, incn=16)
        dcn_all = dcn.now()

        D_("scalar", lambda e: e.activation(sp4[:, 0:ND * NST], asc[:], AF.Exp), waits=[dcn_all], inc=ps_)
        D_("vector", lambda e: e.tensor_scalar(asc[:], sp4[:, 0:ND * NST], -1.0, None, ALU.mult),
           waits=[ps_.now()], inc=pv)
        D_("vector", lambda e: e.tensor_copy(bvb[:], bv_s[:]), waits=[dcn_all], inc=pv)
        D_("vector", lambda e: e.memset(states[:], 0.0), inc=pv)
        D_("vector", lambda e: e.memset(halo[:], 0.0), inc=pv)

        stage_gate = [0]

        def transpose_evac(src_ap, dst_ap, ident, extra_w=()):
            bname = pT_next()
            pT = pTa if bname == "pTa" else pTb
            D_("tensor", lambda e, pT=pT: e.transpose(pT[:, 0:src_ap.shape[0]], src_ap, ident),
               waits=[bk_acq(bname)] + list(extra_w), inc=pt_)
            D_("scalar", lambda e, pT=pT: e.copy(dst_ap, pT[:, 0:src_ap.shape[0]]),
               waits=[pt_.now()], inc=ps_)
            bk_rel(bname, ps_)

        def wprep(dst_sel, src, rows, row_len, tag):
            nkt = row_len // 128
            for r in range(rows):
                slot = r % 2
                st_f = xa[:, slot * 2 * D: slot * 2 * D + row_len]
                st_b = xnT[:, slot * D: slot * D + row_len]
                D_("sync", lambda e, st_f=st_f, r=r: dma(e, st_f, src[r * 128:(r + 1) * 128, :]),
                   waits=[(pt_, stage_gate[0]), (dwt, dwt.n)], inc=dwt, incn=16)
                D_("vector", lambda e, st_b=st_b, st_f=st_f: e.tensor_copy(st_b, st_f),
                   waits=[dwt.now(), (pt_, stage_gate[0])], inc=pv)
                pv_mark = pv.now()
                for k in range(nkt):
                    dst, dcol = dst_sel(r, k)
                    transpose_evac(st_b[:, k * 128:(k + 1) * 128], dst[:, dcol:dcol + 128],
                                   eye_b[:], extra_w=[pv_mark, dcn_all])
                stage_gate[0] = pt_.n

        wprep(lambda r, k: (winT[k], r * 128), win, NE, D, "w_in")
        wprep(lambda r, k: (woutT[k], r * 128), wout, ND, D, "w_out")

        D_("sync", lambda e: dma(e, xa[0:96, 0:E], wx[:, :]),
           waits=[(pt_, stage_gate[0]), (dwt, dwt.n)], inc=dwt, incn=16)
        D_("vector", lambda e: e.tensor_copy(xnT[0:96, 0:E], xa[0:96, 0:E]),
           waits=[dwt.now(), (pt_, stage_gate[0])], inc=pv)
        pv_m = pv.now()
        for k in range(NU):
            transpose_evac(xnT[0:96, k * 128:(k + 1) * 128], wxT[:, k * 96:(k + 1) * 96],
                           eye_b[0:96, 0:96], extra_w=[pv_m, dcn_all])
        stage_gate[0] = pt_.n
        for j in range(ND):
            slot = j % 2
            st_f = xa[:, slot * 2 * D: slot * 2 * D + 64]
            st_b = xnT[:, slot * D: slot * D + 64]
            D_("sync", lambda e, st_f=st_f, j=j: dma(e, st_f, wdt[j * 128:(j + 1) * 128, :]),
               waits=[(pt_, stage_gate[0]), (dwt, dwt.n)], inc=dwt, incn=16)
            D_("vector", lambda e, st_b=st_b, st_f=st_f: e.tensor_copy(st_b, st_f),
               waits=[dwt.now(), (pt_, stage_gate[0])], inc=pv)
            pv_m = pv.now()
            bname = pT_next()
            pT = pTa if bname == "pTa" else pTb
            D_("tensor", lambda e, pT=pT, st_b=st_b: e.transpose(pT[0:64, 0:128], st_b, eye_b[:]),
               waits=[bk_acq(bname), pv_m, dcn_all], inc=pt_)
            mark[("w_dt", j)] = pt_.n
            D_("scalar", lambda e, pT=pT, j=j: e.copy(wdtT[0:64, j * 128:(j + 1) * 128], pT[0:64, 0:128]),
               waits=[pt_.now()], inc=ps_)
            bk_rel(bname, ps_)
            stage_gate[0] = pt_.n

        for eidx in range(NE):
            for k in range(NK):
                D_("tensor", lambda e, eidx=eidx, k=k:
                   e.matmul(pM[:, 0:1], winT[k][:, eidx * 128:(eidx + 1) * 128], bvb[:, k:k + 1],
                            start=(k == 0), stop=(k == NK - 1)),
                   waits=[bk_acq("pM"), (pv, pv.n), (ps_, ps_.n)] if k == 0 else (),
                   inc=pt_ if k == NK - 1 else None)
            D_("scalar", lambda e, eidx=eidx: e.copy(wb_s[:, eidx:eidx + 1], pM[:, 0:1]),
               waits=[pt_.now()], inc=ps_)
            bk_rel("pM", ps_)

        P.base = [dcn_all, (ps_, ps_.n), (pv, pv.n)]
        prep_t = (pt_, pt_.n)

        def x_dma(c):
            if c >= NC:
                return
            slot = c % 2
            dxs_ = dxp[c % 2]
            w = [(pt_, mark.get(("xT", c - 2), 0)), prep_t]
            for tau in range(2):
                D_("sync", lambda e, c=c, slot=slot, tau=tau:
                   dma(e, xa[:, slot * 2 * D + tau * D: slot * 2 * D + (tau + 1) * D],
                       xd[c * TP + tau * 128: c * TP + (tau + 1) * 128, :]),
                   waits=w, inc=dxs_, incn=16)
            mark[("xdma", c)] = (dxs_, dxs_.n)

        def chain(c):
            slot = c % 2

            def sl_ln():
                for tau in range(2):
                    xt = xa[:, slot * 2 * D + tau * D: slot * 2 * D + (tau + 1) * D]
                    col = slot * 2 + tau
                    D_("scalar", lambda e, xt=xt, col=col:
                       e.activation(sp4[:, 0:D], xt, AF.Square, accum_out=ln_q[:, col:col + 1]),
                       waits=[mark[("xdma", c)], ps_.now(),
                              (pv, mark.get(("dtadd", c - 1, ND - 1), 0))], inc=ps_)
                    D_("vector", lambda e, xt=xt, col=col:
                       e.tensor_reduce(ln_s[:, col:col + 1], xt, AX.X, ALU.add),
                       waits=[mark[("xdma", c)]], inc=pv)
                cs = slice(slot * 2, slot * 2 + 2)
                D_("vector", lambda e, cs=cs: e.tensor_scalar(ln_m[:, cs], ln_s[:, cs], 1.0 / D, None, ALU.mult),
                   waits=[ps_.now(), pv.now()], inc=pv)
                D_("vector", lambda e, cs=cs: e.tensor_scalar(ln_v[:, cs], ln_q[:, cs], 1.0 / D, None, ALU.mult),
                   waits=[pv.now()], inc=pv)
                D_("vector", lambda e, cs=cs: e.tensor_tensor(ln_t[:, cs], ln_m[:, cs], ln_m[:, cs], ALU.mult),
                   waits=[pv.now()], inc=pv)
                D_("vector", lambda e, cs=cs: e.tensor_tensor(ln_v[:, cs], ln_v[:, cs], ln_t[:, cs], ALU.subtract),
                   waits=[pv.now()], inc=pv)
                D_("vector", lambda e, cs=cs: e.tensor_scalar(ln_t[:, cs], ln_v[:, cs], 1.0 + EPS, EPS * EPS, ALU.mult, ALU.add),
                   waits=[pv.now()], inc=pv)
                D_("scalar", lambda e, cs=cs: e.activation(ln_v[:, cs], ln_t[:, cs], AF.Ln), waits=[pv.now()], inc=ps_)
                D_("scalar", lambda e, cs=cs: e.activation(q_s[:, cs], ln_v[:, cs], AF.Exp, scale=-0.5), inc=ps_)
                D_("vector", lambda e, cs=cs: e.tensor_tensor(nmq[:, cs], ln_m[:, cs], q_s[:, cs], ALU.mult),
                   waits=[ps_.now(), pv.now()], inc=pv)
                D_("vector", lambda e, cs=cs: e.tensor_scalar(ln_t[:, cs], nmq[:, cs], -1.0, None, ALU.mult),
                   waits=[pv.now()], inc=pv)
                for tau in range(2):
                    xt = xa[:, slot * 2 * D + tau * D: slot * 2 * D + (tau + 1) * D]
                    col = slot * 2 + tau
                    D_("vector", lambda e, xt=xt, col=col:
                       e.tensor_scalar(xt, xt, q_s[:, col:col + 1], ln_t[:, col:col + 1], ALU.mult, ALU.add),
                       waits=[pv.now()], inc=pv)
                mark[("norm", c)] = pv.n

            def sl_xt():
                for tau in range(2):
                    for k in range(NK):
                        src = xa[:, slot * 2 * D + tau * D + k * 128: slot * 2 * D + tau * D + (k + 1) * 128]
                        bname = pF_next()
                        pT = pA if bname == "pA" else pB
                        D_("tensor", lambda e, pT=pT, src=src: e.transpose(pT[:, 0:128], src, eye_f[:]),
                           waits=[bk_acq(bname), (pv, mark[("norm", c)])], inc=pt_)
                        D_("scalar", lambda e, pT=pT, tau=tau, k=k:
                           e.activation(xnT[:, k * TP + tau * 128: k * TP + tau * 128 + 128],
                                        pT[:, 0:128], AF.Identity, scale=gv_s[:, k:k + 1]),
                           waits=[pt_.now(), (pt_, mark.get(("win_mm", c - 1), 0))], inc=ps_)
                        bk_rel(bname, ps_)
                mark[("xT", c)] = pt_.n
                mark[("xnT", c)] = ps_.n

            def win_etile(eidx):
                bname = ("pA", "pB")[eidx % 2]
                pp = pA if bname == "pA" else pB
                for k in range(NK):
                    w = [(ps_, mark[("xnT", c)]), bk_acq(bname)] if k == 0 else ()
                    D_("tensor", lambda e, pp=pp, eidx=eidx, k=k:
                       e.matmul(pp[:, 0:TP], winT[k][:, eidx * 128:(eidx + 1) * 128],
                                xnT[:, k * TP:(k + 1) * TP], start=(k == 0), stop=(k == NK - 1)),
                       waits=w, inc=pt_ if k == NK - 1 else None)
                mm_mark = pt_.n
                if eidx == NE - 1:
                    mark[("win_mm", c)] = pt_.n
                if eidx < NU:
                    ue = uex[:, (eidx % 2) * (TP + 3): (eidx % 2 + 1) * (TP + 3)]
                    ua = uacc[:, (eidx % 2) * TP: (eidx % 2 + 1) * TP]
                    D_("vector", lambda e, ue=ue, eidx=eidx:
                       e.tensor_copy(ue[:, 0:3], halo[:, 3 * eidx:3 * eidx + 3]),
                       waits=[pv.now()], inc=pv)
                    D_("vector", lambda e, ue=ue, pp=pp, eidx=eidx:
                       e.tensor_scalar(ue[:, 3:3 + TP], pp[:, 0:TP], wb_s[:, eidx:eidx + 1], None, ALU.add),
                       waits=[(pt_, mm_mark)], inc=pv)
                    bk_rel(bname, pv)
                    D_("vector", lambda e, ue=ue, eidx=eidx:
                       e.tensor_scalar(ctmp[:], ue[:, 0:TP], cw_s[:, 4 * eidx:4 * eidx + 1], None, ALU.mult),
                       waits=[(ps_, mark.get(("ub", c, eidx - 2), 0))], inc=pv)
                    D_("vector", lambda e, ua=ua, ue=ue, eidx=eidx:
                       e.scalar_tensor_tensor(ua, ue[:, 1:1 + TP], cw_s[:, 4 * eidx + 1:4 * eidx + 2],
                                              ctmp[:], ALU.mult, ALU.add), inc=pv)
                    D_("vector", lambda e, ua=ua, ue=ue, eidx=eidx:
                       e.scalar_tensor_tensor(ctmp[:], ue[:, 2:2 + TP], cw_s[:, 4 * eidx + 2:4 * eidx + 3],
                                              ua, ALU.mult, ALU.add), inc=pv)
                    D_("vector", lambda e, ua=ua, ue=ue, eidx=eidx:
                       e.scalar_tensor_tensor(ua, ue[:, 3:3 + TP], cw_s[:, 4 * eidx + 3:4 * eidx + 4],
                                              ctmp[:], ALU.mult, ALU.add), inc=pv)
                    D_("vector", lambda e, ue=ue, eidx=eidx:
                       e.tensor_copy(halo[:, 3 * eidx:3 * eidx + 3], ue[:, TP:TP + 3]),
                       waits=[pv.now()], inc=pv)
                    mark[("conv", c, eidx)] = pv.n
                    if eidx < ND:
                        j = eidx
                        D_("scalar", lambda e, ua=ua, j=j:
                           e.activation(uh[:, slot * ND * TP + j * TP: slot * ND * TP + (j + 1) * TP],
                                        ua, AF.Silu, bias=cb_s[:, j:j + 1]),
                           waits=[(pv, mark[("conv", c, eidx)]),
                                  (pv, mark.get(("yD", c - 2, j), 0)),
                                  (pv, mark.get(("dt", c - 2, j), 0))], inc=ps_)
                        mark[("uh", c, j)] = ps_.n
                        mark[("ub", c, eidx)] = ps_.n
                    else:
                        D_("scalar", lambda e, ua=ua, eidx=eidx:
                           e.activation(ub_s[:, (eidx - ND) * TP:(eidx - ND + 1) * TP], ua, AF.Silu,
                                        bias=cb_s[:, eidx:eidx + 1]),
                           waits=[(pv, mark[("conv", c, eidx)]),
                                  (pt_, mark.get(("wx_mm", c - 1), 0))], inc=ps_)
                        mark[("ub", c, eidx)] = ps_.n
                else:
                    zi = eidx - NU
                    D_("scalar", lambda e, pp=pp, eidx=eidx, zi=zi:
                       e.activation(szs[:, slot * ND * TP + zi * TP: slot * ND * TP + (zi + 1) * TP],
                                    pp[:, 0:TP], AF.Silu, bias=wb_s[:, eidx:eidx + 1]),
                       waits=[(pt_, mm_mark), (pv, mark.get(("yg", c - 2, zi), 0))], inc=ps_)
                    bk_rel(bname, ps_)
                    if zi == ND - 1:
                        mark[("sz", c)] = ps_.n

            def sl_wx():
                for k in range(NU):
                    w = [(ps_, mark[("ub", c, k)])]
                    if k == 0:
                        w.append(bk_acq("pWx"))
                    rhs = (uh[:, slot * ND * TP + k * TP: slot * ND * TP + (k + 1) * TP]
                           if k < ND else ub_s[:, (k - ND) * TP:(k - ND + 1) * TP])
                    D_("tensor", lambda e, k=k, rhs=rhs:
                       e.matmul(pWx[0:96, 0:TP], wxT[:, k * 96:(k + 1) * 96],
                                rhs, start=(k == 0), stop=(k == NU - 1)),
                       waits=w, inc=pt_ if k == NU - 1 else None)
                mark[("wx_mm", c)] = pt_.n
                D_("scalar", lambda e: e.copy(dtr_b[0:64, :], pWx[0:64, 0:TP]),
                   waits=[(pt_, mark[("wx_mm", c)])], inc=ps_)
                mark[("dtr", c)] = ps_.n
                D_("scalar", lambda e: e.copy(bc_sb[64:96, :], pWx[64:96, 0:TP]),
                   waits=[(bcw, mark.get(("bcw", c - 1), 0))], inc=ps_)
                bk_rel("pWx", ps_)
                mark[("bc", c)] = ps_.n
                D_("sync", lambda e: dma(e, bscr[:, c * TP:(c + 1) * TP], bc_sb[64:96, :]),
                   waits=[(ps_, mark[("bc", c)])], inc=bcw, incn=16)
                mark[("bcw", c)] = bcw.n

            def wdt_tile(j):
                bname = ("pA", "pB")[j % 2]
                pp = pA if bname == "pA" else pB
                D_("tensor", lambda e, pp=pp, j=j:
                   e.matmul(pp[:, 0:TP], wdtT[0:64, j * 128:(j + 1) * 128], dtr_b[0:64, :],
                            start=True, stop=True),
                   waits=[(ps_, mark[("dtr", c)]), bk_acq(bname)], inc=pt_)
                mmk = pt_.n
                s0 = sp4[:, 0:TP]
                s1 = sp4[:, TP:2 * TP]
                D_("scalar", lambda e, pp=pp, j=j, s0=s0:
                   e.activation(s0, pp[:, 0:TP], AF.Abs, bias=bdt_s[:, j:j + 1]),
                   waits=[(pt_, mmk), (pv, mark.get(("dtadd", c, j - 1), 0))], inc=ps_)
                D_("scalar", lambda e, s0=s0, s1=s1: e.activation(s1, s0, AF.Exp, scale=-1.0), inc=ps_)
                D_("scalar", lambda e, s0=s0, s1=s1: e.activation(s0, s1, AF.Ln, bias=1.0), inc=ps_)
                D_("scalar", lambda e, pp=pp, j=j, s1=s1:
                   e.activation(s1, pp[:, 0:TP], AF.Relu, bias=bdt_s[:, j:j + 1]), inc=ps_)
                mark[("sp", c, j)] = ps_.n
                bk_rel(bname, ps_)
                dslot = c % 2
                dts = dt_s[:, dslot * ND * TP + j * TP: dslot * ND * TP + (j + 1) * TP]
                dtus = dtu_s[:, dslot * ND * TP + j * TP: dslot * ND * TP + (j + 1) * TP]
                D_("vector", lambda e, s0=s0, s1=s1, dts=dts: e.tensor_tensor(dts, s0, s1, ALU.add),
                   waits=[(ps_, mark[("sp", c, j)]), (ps_, mark.get(("aexp", c - 1), 0))], inc=pv)
                mark[("dtadd", c, j)] = pv.n
                D_("vector", lambda e, dts=dts, dtus=dtus, j=j:
                   e.tensor_tensor(dtus, dts, uh[:, dslot * ND * TP + j * TP: dslot * ND * TP + (j + 1) * TP],
                                   ALU.mult),
                   waits=[(ps_, mark[("uh", c, j)])], inc=pv)
                mark[("dt", c, j)] = pv.n

            def sl_reps():
                wrep = [(bcw, mark[("bcw", c)]), (pv, mark.get(("repfree", c - 1), 0))]
                D_("sync", lambda e:
                   dma(e, repB[:].rearrange("p (n t) -> p n t", n=NST),
                       bscr[0:NST, c * TP:(c + 1) * TP][None].broadcast_to((128, NST, TP))),
                   waits=wrep, inc=rep, incn=16)
                D_("sync", lambda e:
                   dma(e, repC[:].rearrange("p (n t) -> p n t", n=NST),
                       bscr[NST:2 * NST, c * TP:(c + 1) * TP][None].broadcast_to((128, NST, TP))),
                   waits=wrep, inc=rep, incn=16)
                mark[("rep", c)] = rep.n

            th = [sl_ln, sl_xt]
            for i in range(NE):
                th.append(lambda i=i: win_etile(i))
            th.append(sl_wx)
            for j in range(ND):
                th.append(lambda j=j: wdt_tile(j))
            th.append(sl_reps)
            return th

        def scan_dtile(c, j):
            dslot = c % 2
            dts = dt_s[:, dslot * ND * TP + j * TP: dslot * ND * TP + (j + 1) * TP]
            dtus = dtu_s[:, dslot * ND * TP + j * TP: dslot * ND * TP + (j + 1) * TP]
            D_("vector", lambda e, dtus=dtus:
               e.tensor_tensor(ball[:].rearrange("p (n t) -> p n t", n=NST),
                               dtus[:, None].broadcast_to((128, NST, TP)),
                               repB[:].rearrange("p (n t) -> p n t", n=NST), ALU.mult),
               waits=[(rep, mark[("rep", c)])], inc=pv)
            mark[("p1", c, j)] = pv.n
            for n in range(NST):
                g = (c * ND + j) * NST + n
                aslot = a_s[:, (n % 3) * TP:(n % 3 + 1) * TP]
                D_("scalar", lambda e, dts=dts, j=j, n=n, aslot=aslot:
                   e.activation(aslot, dts, AF.Exp, scale=asc[:, j * NST + n: j * NST + n + 1]),
                   waits=[(pv, mark[("dt", c, j)]), (pv, ttsm.get(g - 3, 0))], inc=ps_)
                am = ps_.n
                D_("vector", lambda e, n=n, j=j, aslot=aslot:
                   e.tensor_tensor_scan(hall[:, n * TP:(n + 1) * TP], aslot, ball[:, n * TP:(n + 1) * TP],
                                        states[:, n * ND + j: n * ND + j + 1], ALU.mult, ALU.add),
                   waits=[(ps_, am)], inc=pv)
                ttsm[g] = pv.n
            if j == ND - 1:
                mark[("aexp", c)] = ps_.n
            D_("vector", lambda e, j=j:
               e.tensor_copy(states[:, j: j + (NST - 1) * ND + 1: ND], hall[:, TP - 1:: TP]),
               waits=[pv.now()], inc=pv)
            D_("vector", lambda e:
               e.tensor_tensor(zall[:].rearrange("p (n t) -> p n t", n=NST),
                               hall[:].rearrange("p (n t) -> p n t", n=NST),
                               repC[:].rearrange("p (n t) -> p n t", n=NST), ALU.mult),
               inc=pv)
            if j == ND - 1:
                mark[("repfree", c)] = pv.n
            D_("vector", lambda e:
               e.tensor_tensor(ball[:, 0:8 * TP], zall[:, 0:8 * TP], zall[:, 8 * TP:16 * TP], ALU.add),
               inc=pv)
            D_("vector", lambda e:
               e.tensor_tensor(ball[:, 8 * TP:12 * TP], ball[:, 0:4 * TP], ball[:, 4 * TP:8 * TP], ALU.add),
               inc=pv)
            D_("vector", lambda e:
               e.tensor_tensor(ball[:, 12 * TP:14 * TP], ball[:, 8 * TP:10 * TP], ball[:, 10 * TP:12 * TP], ALU.add),
               inc=pv)
            D_("vector", lambda e:
               e.tensor_tensor(ctmp[:], ball[:, 12 * TP:13 * TP], ball[:, 13 * TP:14 * TP], ALU.add),
               waits=[pv.now()], inc=pv)
            D_("vector", lambda e, j=j:
               e.scalar_tensor_tensor(y_t[:], uh[:, dslot * ND * TP + j * TP: dslot * ND * TP + (j + 1) * TP],
                                      dsk_s[:, j:j + 1], ctmp[:], ALU.mult, ALU.add),
               waits=[(ps_, mark[("uh", c, j)])], inc=pv)
            mark[("yD", c, j)] = pv.n
            D_("vector", lambda e, j=j, dslot=dslot:
               e.tensor_tensor(yg_s[:, dslot * ND * TP + j * TP: dslot * ND * TP + (j + 1) * TP], y_t[:],
                               szs[:, dslot * ND * TP + j * TP: dslot * ND * TP + (j + 1) * TP], ALU.mult),
               waits=[(pt_, mark.get(("wout_mm", c - 2), 0)), (ps_, mark[("sz", c)])], inc=pv)
            mark[("yg", c, j)] = pv.n

        def emit_out(c):
            for dm in range(ND):
                bname = ("pA", "pB")[dm % 2]
                pp = pA if bname == "pA" else pB
                oslot = c % 2
                for k in range(ND):
                    w = [(pv, mark[("yg", c, ND - 1)]), bk_acq(bname)] if k == 0 else ()
                    D_("tensor", lambda e, pp=pp, dm=dm, k=k, oslot=oslot:
                       e.matmul(pp[:, 0:TP], woutT[k][:, dm * 128:(dm + 1) * 128],
                                yg_s[:, oslot * ND * TP + k * TP: oslot * ND * TP + (k + 1) * TP],
                                start=(k == 0), stop=(k == ND - 1)),
                       waits=w, inc=pt_ if k == ND - 1 else None)
                mmk = pt_.n
                if dm == ND - 1:
                    mark[("wout_mm", c)] = pt_.n
                ofs = ofm[:, (dm % 2) * TP:(dm % 2 + 1) * TP]
                w_ofm = mark.get(("otT", c, dm - 2), mark.get(("otT", c - 1, ND + dm - 2), 0))
                D_("scalar", lambda e, pp=pp, ofs=ofs:
                   e.copy(ofs, pp[:, 0:TP]),
                   waits=[(pt_, mmk), (pt_, w_ofm)], inc=ps_)
                bk_rel(bname, ps_)
                ofm_m = ps_.n
                for tau in range(2):
                    bnT = pF_next()
                    pT = pA if bnT == "pA" else pB
                    D_("tensor", lambda e, pT=pT, ofs=ofs, tau=tau:
                       e.transpose(pT[:, 0:128], ofs[:, tau * 128:(tau + 1) * 128], eye_f[:]),
                       waits=[bk_acq(bnT), (ps_, ofm_m)], inc=pt_)
                    D_("scalar", lambda e, pT=pT, tau=tau, dm=dm:
                       e.copy(otm[:, tau * D + dm * 128: tau * D + (dm + 1) * 128], pT[:, 0:128]),
                       waits=[pt_.now(), (po, 32 * c)], inc=ps_)
                    bk_rel(bnT, ps_)
                    mark[("otm_e", c, dm, tau)] = ps_.n
                mark[("otT", c, dm)] = pt_.n
            for tau in range(2):
                D_("sync", lambda e, c=c, tau=tau:
                   dma(e, part[c * TP + tau * 128: c * TP + (tau + 1) * 128, :],
                       otm[:, tau * D:(tau + 1) * D]),
                   waits=[(ps_, mark[("otm_e", c, ND - 1, tau)])], inc=po, incn=16)

        x_dma(0)
        x_dma(1)
        for t in chain(0):
            t()
        for c in range(NC):
            x_dma(c + 2)
            th_next = chain(c + 1) if c + 1 < NC else None
            for j in range(ND):
                scan_dtile(c, j)
                if j == 1:
                    if th_next is not None:
                        for t in th_next:
                            t()
                    if c >= 1:
                        emit_out(c - 1)
            if c == NC - 1:
                emit_out(c)

        if DBG[0]:
            dbg_specs = [("d_xnT", xnT, None), ("d_ub", ub_s, None), ("d_uh", uh, ND * TP),
                         ("d_dt", dt_s, ND * TP), ("d_dtu", dtu_s, ND * TP), ("d_szs", szs, ND * TP),
                         ("d_repB", repB, None), ("d_repC", repC, None),
                         ("d_hall", hall, None), ("d_yg", yg_s, None), ("d_states", states, None),
                         ("d_ball", ball, None), ("d_asc", asc, None), ("d_wb", wb_s, None),
                         ("d_xa", xa, 2 * D)]
            for nm, tile, w_ in dbg_specs:
                w_ = w_ or tile.shape[1]
                od = nc.dram_tensor(nm, [128, w_], tile.dtype, kind="ExternalOutput").ap()
                D_("sync", lambda e, od=od, tile=tile, w_=w_: dma(e, od[:, :], tile[0:128, 0:w_]),
                   waits=[(po, po.n)], inc=po, incn=16)
        D_("sync", lambda e: e.nop(), waits=[(po, po.n)])
        P.simulate()
        block = cm.enter_context(nc.Block())
        P.attach(block)
    return nc


DBG = [False]



def build_l2(TOK=2048, TC=128):
    NCH = TOK // TC
    NT = TC // 128
    nc = bass.Bass("TRN2", target_bir_lowering=False, debug=False, detect_race_conditions=False)
    dr = lambda n, s, dt, k: nc.dram_tensor(n, s, dt, kind=k).ap()
    xf = dr("xf", [TOK, D], F32, "ExternalInput")
    pr = [dr(f"p{i}", [TOK, D], F32, "ExternalInput") for i in range(4)]
    w1 = dr("w1", [HF, D], F32, "ExternalInput")
    b1 = dr("b1", [HF, 1], F32, "ExternalInput")
    w2 = dr("w2", [D, HF], F32, "ExternalInput")
    b2 = dr("b2", [D, 1], F32, "ExternalInput")
    eyeb = dr("eyeb", [128, 128], BF16, "ExternalInput")
    out = dr("o", [TOK, D], F32, "ExternalOutput")

    NH = HF // 128
    NK = D // 128

    cm = contextlib.ExitStack()
    with cm:
        bw1T = [_sb(nc, cm, f"bw1T{k}", [128, HF], BF16) for k in range(NK)]
        bw2T = [_sb(nc, cm, f"bw2T{k}", [128, D], BF16) for k in range(NH)]
        al1 = _sb(nc, cm, "al1", [128, NH], F32)
        al2p = _sb(nc, cm, "al2p", [128, 4 * NK], F32)
        al2 = _sb(nc, cm, "al2", [128, NK], F32)
        al2tA = _sb(nc, cm, "al2tA", [128, NK], F32)
        al2tB = _sb(nc, cm, "al2tB", [128, NK], F32)
        al2r = _sb(nc, cm, "al2r", [128, NK], F32)
        b1_s = _sb(nc, cm, "b1_s", [128, NH], F32)
        b2_s = _sb(nc, cm, "b2_s", [128, NK], F32)
        eye_b = _sb(nc, cm, "eye_b", [128, 128], BF16)
        wst = _sb(nc, cm, "wst", [128, 2 * D], F32)
        wsb = _sb(nc, cm, "wsb", [128, 2 * D], BF16)
        xs = _sb(nc, cm, "xs", [128, 2 * NT * D], F32)
        racc = _sb(nc, cm, "racc", [128, 2 * NT * D], F32)
        r1 = _sb(nc, cm, "r1", [128, 2 * NT * D], F32)
        sqg = _sb(nc, cm, "sqg", [128, D], F32)
        ln_s = _sb(nc, cm, "ln_s", [128, 2 * NT], F32)
        ln_q = _sb(nc, cm, "ln_q", [128, 2 * NT], F32)
        ln_m = _sb(nc, cm, "ln_m", [128, 2 * NT], F32)
        ln_v = _sb(nc, cm, "ln_v", [128, 2 * NT], F32)
        ln_t = _sb(nc, cm, "ln_t", [128, 2 * NT], F32)
        q1_s = _sb(nc, cm, "q1_s", [128, 2 * NT], F32)
        s1a = _sb(nc, cm, "s1a", [128, 2 * NT], F32)
        s1b = _sb(nc, cm, "s1b", [128, 2 * NT], F32)
        xn2 = _sb(nc, cm, "xn2", [128, NT * D], BF16)
        xnT2 = _sb(nc, cm, "xnT2", [128, NK * TC], BF16)
        g_all = _sb(nc, cm, "g_all", [128, NH * TC], BF16)
        ffm = _sb(nc, cm, "ffm", [128, 2 * TC], BF16)
        otm = _sb(nc, cm, "otm", [128, NT * D], F32)
        zzpad = _sb(nc, cm, "zzpad", [128, 24], F32)
        pA = _ps(nc, cm, "pA")
        pB = _ps(nc, cm, "pB")
        pTa = _ps(nc, cm, "pTa", BF16)
        pTb = _ps(nc, cm, "pTb", BF16)

        P = Plan()
        S = {k: Sem(nc, cm, k) for k in ("dx0", "dx1", "dwt", "dac", "po", "ps", "pv", "pt", "pg")}
        dxq = (S["dx0"], S["dx1"])
        dwt, dac, po = (S[k] for k in ("dwt", "dac", "po"))
        ps_, pv, pt_, pg = (S[k] for k in ("ps", "pv", "pt", "pg"))
        mark = {}
        banks = {b: (ps_, 0) for b in ("pA", "pB", "pTa", "pTb")}

        def bk_acq(b):
            s, v = banks[b]
            return (s, v)

        def bk_rel(b, sem):
            banks[b] = (sem, sem.n)

        tcyc = [0]

        def D_(eng, fn, waits=(), inc=None, incn=1):
            P.do(eng, fn, waits, inc, incn)

        def dma(e, dst, src):
            return e.dma_start(out=dst, in_=src)

        D_("sync", lambda e: dma(e, eye_b[:], eyeb[:]), inc=dwt, incn=16)
        for h in range(NH):
            D_("sync", lambda e, h=h: dma(e, b1_s[:, h:h + 1], b1[h * 128:(h + 1) * 128, :]), inc=dwt, incn=16)
        for k in range(NK):
            D_("sync", lambda e, k=k: dma(e, b2_s[:, k:k + 1], b2[k * 128:(k + 1) * 128, :]), inc=dwt, incn=16)
        dwt_c = dwt.now()

        def bin_block(src, r0, col0, alpha_dst, dst_sel, tag, ti):
            slot = ti % 2
            st_f = wst[:, slot * D: slot * D + D]
            st_b = wsb[:, slot * D: slot * D + D]
            D_("sync", lambda e, st_f=st_f: dma(e, st_f, src[r0:r0 + 128, col0:col0 + D]),
               waits=[(ps_, mark.get((tag, ti - 2), 0)), (pv, mark.get((tag, "v", ti - 2), 0)),
                      (dwt, dwt.n)], inc=dwt, incn=16)
            D_("vector", lambda e, st_f=st_f, alpha_dst=alpha_dst:
               e.tensor_reduce(alpha_dst, st_f, AX.X, ALU.add, apply_absolute_value=True),
               waits=[dwt.now()], inc=pv)
            mark[(tag, "v", ti)] = pv.n
            D_("scalar", lambda e, st_f=st_f, st_b=st_b: e.activation(st_b, st_f, AF.Sign),
               waits=[dwt.now(), pv.now()], inc=ps_)
            sg = ps_.now()
            for k in range(NK):
                bname = ("pTa", "pTb")[tcyc[0] % 2]
                tcyc[0] += 1
                pT = pTa if bname == "pTa" else pTb
                dst, dcol = dst_sel(k)
                D_("tensor", lambda e, pT=pT, st_b=st_b, k=k:
                   e.transpose(pT[:, 0:128], st_b[:, k * 128:(k + 1) * 128], eye_b[:]),
                   waits=[bk_acq(bname), sg, dwt_c], inc=pt_)
                D_("scalar", lambda e, pT=pT, dst=dst, dcol=dcol:
                   e.copy(dst[:, dcol:dcol + 128], pT[:, 0:128]),
                   waits=[pt_.now()], inc=ps_)
                bk_rel(bname, ps_)
            mark[(tag, ti)] = ps_.n

        ti = 0
        for r in range(NH):
            bin_block(w1, r * 128, 0, al1[:, r:r + 1], lambda k, r=r: (bw1T[k], r * 128), "bb", ti)
            ti += 1
        for r in range(NK):
            for bcol in range(4):
                bin_block(w2, r * 128, bcol * D, al2p[:, bcol * NK + r:bcol * NK + r + 1],
                          lambda k, r=r, bcol=bcol: (bw2T[bcol * 8 + k], r * 128), "bb", ti)
                ti += 1
        D_("vector", lambda e: e.tensor_copy(al2tA[:, 0:NK], al1[:, 0:NK]), inc=pv)
        D_("vector", lambda e: e.tensor_scalar(al1[:], al1[:], 1.0 / D, None, ALU.mult), waits=[pv.now()], inc=pv)
        D_("vector", lambda e: e.tensor_tensor(al2tA[:], al2p[:, 0:NK], al2p[:, NK:2 * NK], ALU.add), waits=[pv.now()], inc=pv)
        D_("vector", lambda e: e.tensor_tensor(al2tB[:], al2p[:, 2 * NK:3 * NK], al2p[:, 3 * NK:4 * NK], ALU.add), waits=[pv.now()], inc=pv)
        D_("vector", lambda e: e.tensor_tensor(al2r[:], al2tA[:], al2tB[:], ALU.add), waits=[pv.now()], inc=pv)
        D_("vector", lambda e: e.tensor_scalar(al2[:], al2r[:], 1.0 / HF, None, ALU.mult), waits=[pv.now()], inc=pv)
        P.base = [dwt_c, (pv, pv.n)]

        def xp_dma(c):
            if c >= NCH:
                return
            slot = c % 2
            w = [(pv, mark.get(("r1done", c - 2), 0)), (po, max(0, (c - 1)) * NT * 16)]
            dxs_ = dxq[c % 2]
            D_("sync", lambda e, c=c, slot=slot:
               dma(e, xs[:, slot * NT * D:(slot + 1) * NT * D].rearrange("p (t d) -> p t d", d=D),
                   xf[c * TC:(c + 1) * TC, :].rearrange("(t p) d -> p t d", p=128)),
               waits=w, inc=dxs_, incn=16)
            mark[("xdma", c)] = (dxs_, dxs_.n)
            rs = racc[:, slot * NT * D:(slot + 1) * NT * D].rearrange("p (t d) -> p t d", d=D)
            for i in range(4):
                D_("gpsimd", lambda e, i=i, c=c, rs=rs:
                   e.dma_start(out=rs, in_=pr[i][c * TC:(c + 1) * TC, :].rearrange("(t p) d -> p t d", p=128),
                               accum_op=(ALU.bypass if i == 0 else ALU.add)),
                   waits=w + [(dac, dac.n)], inc=dac, incn=16)
            mark[("pacc", c)] = dac.n

        xp_dma(0)
        xp_dma(1)
        for c in range(NCH):
            slot = c % 2
            xc = xs[:, slot * NT * D:(slot + 1) * NT * D]
            rc = racc[:, slot * NT * D:(slot + 1) * NT * D]
            r1c = r1[:, slot * NT * D:(slot + 1) * NT * D]
            cs = slice(slot * NT, slot * NT + NT)
            for tau in range(NT):
                xt = xc[:, tau * D:(tau + 1) * D]
                col = slot * NT + tau
                D_("scalar", lambda e, xt=xt, col=col:
                   e.activation(sqg[:], xt, AF.Square, accum_out=ln_q[:, col:col + 1]),
                   waits=[mark[("xdma", c)], ps_.now(), pv.now()], inc=ps_)
                D_("vector", lambda e, xt=xt, col=col:
                   e.tensor_reduce(ln_s[:, col:col + 1], xt, AX.X, ALU.add),
                   waits=[mark[("xdma", c)]], inc=pv)
            D_("vector", lambda e, cs=cs: e.tensor_scalar(ln_m[:, cs], ln_s[:, cs], 1.0 / D, None, ALU.mult),
               waits=[ps_.now(), pv.now()], inc=pv)
            D_("vector", lambda e, cs=cs: e.tensor_scalar(ln_v[:, cs], ln_q[:, cs], 1.0 / D, None, ALU.mult), waits=[pv.now()], inc=pv)
            D_("vector", lambda e, cs=cs: e.tensor_tensor(ln_t[:, cs], ln_m[:, cs], ln_m[:, cs], ALU.mult), waits=[pv.now()], inc=pv)
            D_("vector", lambda e, cs=cs: e.tensor_tensor(ln_v[:, cs], ln_v[:, cs], ln_t[:, cs], ALU.subtract), waits=[pv.now()], inc=pv)
            D_("vector", lambda e, cs=cs: e.tensor_scalar(ln_t[:, cs], ln_v[:, cs], 1.0, EPS, ALU.mult, ALU.add), waits=[pv.now()], inc=pv)
            D_("scalar", lambda e, cs=cs: e.activation(ln_v[:, cs], ln_t[:, cs], AF.Ln), waits=[pv.now()], inc=ps_)
            D_("scalar", lambda e, cs=cs: e.activation(q1_s[:, cs], ln_v[:, cs], AF.Exp, scale=-0.5), inc=ps_)
            D_("vector", lambda e, cs=cs: e.tensor_scalar(s1a[:, cs], q1_s[:, cs], 1.0, None, ALU.add),
               waits=[ps_.now(), pv.now()], inc=pv)
            D_("vector", lambda e, cs=cs: e.tensor_tensor(s1b[:, cs], ln_m[:, cs], q1_s[:, cs], ALU.mult), waits=[pv.now()], inc=pv)
            for tau in range(NT):
                col = slot * NT + tau
                D_("vector", lambda e, tau=tau, col=col, xc=xc:
                   e.tensor_scalar(xc[:, tau * D:(tau + 1) * D], xc[:, tau * D:(tau + 1) * D],
                                   s1a[:, col:col + 1], s1b[:, col:col + 1], ALU.mult, ALU.subtract),
                   inc=pv)
            D_("vector", lambda e, r1c=r1c, xc=xc, rc=rc: e.tensor_tensor(r1c, xc, rc, ALU.add),
               waits=[(dac, mark[("pacc", c)])], inc=pv)
            mark[("r1", c)] = pv.n
            for tau in range(NT):
                col = slot * NT + tau
                D_("scalar", lambda e, tau=tau, col=col, r1c=r1c:
                   e.activation(sqg[:], r1c[:, tau * D:(tau + 1) * D], AF.Square, accum_out=ln_q[:, col:col + 1]),
                   waits=[(pv, mark[("r1", c)]), ps_.now(), pv.now()], inc=ps_)
                D_("vector", lambda e, tau=tau, col=col, r1c=r1c:
                   e.tensor_reduce(ln_s[:, col:col + 1], r1c[:, tau * D:(tau + 1) * D], AX.X, ALU.add),
                   inc=pv)
            D_("vector", lambda e, cs=cs: e.tensor_scalar(ln_m[:, cs], ln_s[:, cs], 1.0 / D, None, ALU.mult),
               waits=[ps_.now(), pv.now()], inc=pv)
            D_("vector", lambda e, cs=cs: e.tensor_scalar(ln_v[:, cs], ln_q[:, cs], 1.0 / D, None, ALU.mult), waits=[pv.now()], inc=pv)
            D_("vector", lambda e, cs=cs: e.tensor_tensor(ln_t[:, cs], ln_m[:, cs], ln_m[:, cs], ALU.mult), waits=[pv.now()], inc=pv)
            D_("vector", lambda e, cs=cs: e.tensor_tensor(ln_v[:, cs], ln_v[:, cs], ln_t[:, cs], ALU.subtract), waits=[pv.now()], inc=pv)
            D_("vector", lambda e, cs=cs: e.tensor_scalar(ln_t[:, cs], ln_v[:, cs], 1.0, EPS, ALU.mult, ALU.add), waits=[pv.now()], inc=pv)
            D_("scalar", lambda e, cs=cs: e.activation(ln_v[:, cs], ln_t[:, cs], AF.Ln), waits=[pv.now()], inc=ps_)
            D_("scalar", lambda e, cs=cs: e.activation(q1_s[:, cs], ln_v[:, cs], AF.Exp, scale=-0.5), inc=ps_)
            for tau in range(NT):
                col = slot * NT + tau
                D_("vector", lambda e, tau=tau, col=col, r1c=r1c:
                   e.tensor_scalar(xn2[:, tau * D:(tau + 1) * D], r1c[:, tau * D:(tau + 1) * D],
                                   ln_m[:, col:col + 1], q1_s[:, col:col + 1], ALU.subtract, ALU.mult),
                   waits=[ps_.now(), (ps_, mark.get(("xnTe", c - 1), 0))], inc=pv)
            mark[("xn2", c)] = pv.n
            for tau in range(NT):
                for k in range(NK):
                    bname = ("pTa", "pTb")[tcyc[0] % 2]
                    tcyc[0] += 1
                    pT = pTa if bname == "pTa" else pTb
                    D_("tensor", lambda e, pT=pT, tau=tau, k=k:
                       e.transpose(pT[:, 0:128], xn2[:, tau * D + k * 128: tau * D + (k + 1) * 128], eye_b[:]),
                       waits=[bk_acq(bname), (pv, mark[("xn2", c)])], inc=pt_)
                    D_("scalar", lambda e, pT=pT, tau=tau, k=k:
                       e.copy(xnT2[:, k * TC + tau * 128: k * TC + tau * 128 + 128], pT[:, 0:128]),
                       waits=[pt_.now()], inc=ps_)
                    bk_rel(bname, ps_)
            mark[("xnTe", c)] = ps_.n
            for h in range(NH):
                bname = ("pA", "pB")[h % 2]
                pp = pA if bname == "pA" else pB
                for k in range(NK):
                    w = [(ps_, mark[("xnTe", c)]), bk_acq(bname)] if k == 0 else ()
                    D_("tensor", lambda e, pp=pp, h=h, k=k:
                       e.matmul(pp[:, 0:TC], bw1T[k][:, h * 128:(h + 1) * 128],
                                xnT2[:, k * TC:(k + 1) * TC], start=(k == 0), stop=(k == NK - 1)),
                       waits=w, inc=pt_ if k == NK - 1 else None)
                mmk = pt_.n
                if h == NH - 1:
                    mark[("w1_mm", c)] = pt_.n
                D_("scalar", lambda e, pp=pp, h=h:
                   e.activation(g_all[:, h * TC:(h + 1) * TC], pp[:, 0:TC], AF.Gelu,
                                bias=b1_s[:, h:h + 1], scale=al1[:, h:h + 1]),
                   waits=[(pt_, mmk), (pt_, mark.get(("w2_mm", c - 1), 0))], inc=ps_)
                bk_rel(bname, ps_)
            mark[("gall", c)] = ps_.n
            for dm in range(NK):
                bname = ("pA", "pB")[dm % 2]
                pp = pA if bname == "pA" else pB
                for k in range(NH):
                    w = [(ps_, mark[("gall", c)]), bk_acq(bname)] if k == 0 else ()
                    D_("tensor", lambda e, pp=pp, dm=dm, k=k:
                       e.matmul(pp[:, 0:TC], bw2T[k][:, dm * 128:(dm + 1) * 128],
                                g_all[:, k * TC:(k + 1) * TC], start=(k == 0), stop=(k == NH - 1)),
                       waits=w, inc=pt_ if k == NH - 1 else None)
                mmk = pt_.n
                if dm == NK - 1:
                    mark[("w2_mm", c)] = pt_.n
                ffs = ffm[:, (dm % 2) * TC:(dm % 2 + 1) * TC]
                w_ffm = mark.get(("oT", c, dm - 2), mark.get(("oT", c - 1, NK + dm - 2), 0))
                D_("scalar", lambda e, pp=pp, ffs=ffs, dm=dm:
                   e.activation(ffs, pp[:, 0:TC], AF.Identity, bias=b2_s[:, dm:dm + 1], scale=al2[:, dm:dm + 1]),
                   waits=[(pt_, mmk), (pt_, w_ffm)], inc=ps_)
                bk_rel(bname, ps_)
                ffm_m = ps_.n
                for tau in range(NT):
                    bnT = ("pTa", "pTb")[tcyc[0] % 2]
                    tcyc[0] += 1
                    pT = pTa if bnT == "pTa" else pTb
                    D_("tensor", lambda e, pT=pT, ffs=ffs, tau=tau:
                       e.transpose(pT[:, 0:128], ffs[:, tau * 128:(tau + 1) * 128], eye_b[:]),
                       waits=[bk_acq(bnT), (ps_, ffm_m)], inc=pt_)
                    D_("scalar", lambda e, pT=pT, tau=tau, dm=dm:
                       e.copy(otm[:, tau * D + dm * 128: tau * D + (dm + 1) * 128], pT[:, 0:128]),
                       waits=[pt_.now(), (po, NT * 16 * c)], inc=ps_)
                    bk_rel(bnT, ps_)
                    mark[("ote", c, dm, tau)] = ps_.n
                mark[("oT", c, dm)] = pt_.n
            for tau in range(NT):
                D_("vector", lambda e, tau=tau, r1c=r1c, rc=rc:
                   e.tensor_tensor(rc[:, tau * D:(tau + 1) * D], otm[:, tau * D:(tau + 1) * D],
                                   r1c[:, tau * D:(tau + 1) * D], ALU.add),
                   waits=[(ps_, mark[("ote", c, NK - 1, tau)])], inc=pv)
                mark[("oadd", c, tau)] = pv.n
            mark[("r1done", c)] = pv.n
            for tau in range(NT):
                D_("sync", lambda e, c=c, tau=tau, rc=rc:
                   dma(e, out[c * TC + tau * 128: c * TC + (tau + 1) * 128, :], rc[:, tau * D:(tau + 1) * D]),
                   waits=[(pv, mark[("oadd", c, tau)])], inc=po, incn=16)
            xp_dma(c + 2)

        if DBG[0]:
            dbg2 = [("e_al1", al1), ("e_al2", al2), ("e_al2p", al2p), ("e_al2r", al2r), ("e_al2tA", al2tA), ("e_al2tB", al2tB), ("e_bw1T0", bw1T[0]), ("e_bw2T0", bw2T[0]),
                    ("e_racc", racc), ("e_r1", r1), ("e_xn2", xn2), ("e_xnT2", xnT2),
                    ("e_gall", g_all), ("e_otm", otm), ("e_q1", q1_s), ("e_lnm", ln_m)]
            for nm, tile in dbg2:
                od = nc.dram_tensor(nm, [128, tile.shape[1]], tile.dtype, kind="ExternalOutput").ap()
                D_("sync", lambda e, od=od, tile=tile: dma(e, od[:, :], tile[0:128, :]),
                   waits=[(po, po.n)], inc=po, incn=16)
        D_("sync", lambda e: e.nop(), waits=[(po, po.n)])
        P.simulate()
        block = cm.enter_context(nc.Block())
        P.attach(block)
    return nc



def _l1_in_maps(inp, x, eyef, eyeb):
    f = lambda k: np.ascontiguousarray(np.asarray(inp[k], dtype=np.float32))
    in_maps = []
    for c in range(8):
        b, dire, dh = c >> 2, (c >> 1) & 1, c & 1
        pfx = "f" if dire == 0 else "b"
        xdir = x[b] if dire == 0 else x[b, ::-1]
        winf = f(f"L0_attn_{pfx}_Win")
        u_rows = winf[0:E]
        z_rows = winf[E + dh * EH: E + (dh + 1) * EH]
        cwm = f(f"L0_attn_{pfx}_convw").reshape(E, 4)
        cbm = f(f"L0_attn_{pfx}_convb").reshape(E, 1)
        wxm = f(f"L0_attn_{pfx}_Wx")
        if dh == 1:
            u_rows = np.concatenate([u_rows[EH:], u_rows[0:EH]], axis=0)
            cwm = np.concatenate([cwm[EH:], cwm[0:EH]], axis=0)
            cbm = np.concatenate([cbm[EH:], cbm[0:EH]], axis=0)
            wxm = np.concatenate([wxm[:, EH:], wxm[:, 0:EH]], axis=1)
        m = {
            "xd": np.ascontiguousarray(xdir),
            "win": np.ascontiguousarray(np.concatenate([u_rows, z_rows], axis=0)),
            "convw": np.ascontiguousarray(cwm),
            "convb": np.ascontiguousarray(cbm),
            "wx": np.ascontiguousarray(wxm),
            "wdt": np.ascontiguousarray(f(f"L0_attn_{pfx}_Wdt")[dh * EH:(dh + 1) * EH]),
            "bdt": np.ascontiguousarray(f(f"L0_attn_{pfx}_bdt").reshape(E, 1)[dh * EH:(dh + 1) * EH]),
            "alog": np.ascontiguousarray(f(f"L0_attn_{pfx}_Alog")[dh * EH:(dh + 1) * EH]),
            "dsk": np.ascontiguousarray(f(f"L0_attn_{pfx}_D").reshape(E, 1)[dh * EH:(dh + 1) * EH]),
            "wout": np.ascontiguousarray(f(f"L0_attn_{pfx}_Wout")[:, dh * EH:(dh + 1) * EH]),
            "gvec": f("L0_attn_norm_g").reshape(D, 1),
            "bvec": f("L0_attn_norm_b").reshape(D, 1),
            "eyef": eyef, "eyeb": eyeb,
        }
        in_maps.append(m)
    return in_maps


def kernel(**inp):
    f = lambda k: np.ascontiguousarray(np.asarray(inp[k], dtype=np.float32))
    x = f("L0_attn_x")
    L = x.shape[1]
    eyef = np.eye(128, dtype=np.float32)
    eyeb = np.eye(128).astype(BF16_NP)

    nc1 = build_l1(L)
    in_maps = _l1_in_maps(inp, x, eyef, eyeb)
    res1 = run_bass_kernel_spmd(nc1, in_maps, core_ids=list(range(8)))
    parts = []
    for c in range(8):
        p = np.asarray(res1.results[c]["part"], dtype=np.float32)
        if (c >> 1) & 1:
            p = p[::-1]
        parts.append(np.ascontiguousarray(p))

    nc2 = build_l2()
    in_maps2 = []
    for k in range(8):
        bk = (k * 2048) // L
        lo = k * 2048 - bk * L
        sl = slice(lo, lo + 2048)
        m = {
            "xf": np.ascontiguousarray(x[bk][sl]),
            "w1": f("L0_ffn_w1"),
            "b1": f("L0_ffn_b1").reshape(HF, 1),
            "w2": f("L0_ffn_w2"),
            "b2": f("L0_ffn_b2").reshape(D, 1),
            "eyeb": eyeb,
        }
        for i in range(4):
            m[f"p{i}"] = parts[bk * 4 + i][sl]
        in_maps2.append(m)
    res2 = run_bass_kernel_spmd(nc2, in_maps2, core_ids=list(range(8)))
    outs = [np.asarray(res2.results[k]["o"], dtype=np.float32) for k in range(8)]
    return np.concatenate(outs, axis=0).reshape(2, L, D)
